# version 28
# baseline (speedup 1.0000x reference)
import numpy as np
import ml_dtypes
from contextlib import ExitStack

import concourse.bass as bass
import concourse.mybir as mybir
from concourse.bass_utils import run_bass_kernel_spmd
from concourse.alu_op_type import AluOpType

F32 = mybir.dt.float32
BF16 = mybir.dt.bfloat16
AX = mybir.AxisListType.X
AXC = mybir.AxisListType.C
AF = mybir.ActivationFunctionType
ADD, SUB, MUL = AluOpType.add, AluOpType.subtract, AluOpType.mult
MAXO, MINO = AluOpType.max, AluOpType.min

C1 = float(3 << 22)
C2 = float(3 << 51)
NEG_BIG = -3.0e38
INV32 = 0.03125
INV127 = float(np.float32(np.float64(1.0) / np.float64(127.0)))
INVN = float(np.float32(1.0) / np.float32(1048576.0))
EPS_LN = 1e-5
EPS_Q = 1e-8

B, S, D = 2, 4096, 1024
NTB, TBW = 8, 512
NST = 32
NSB = 8
GSZ = 8
NG = 4


class Prog:
    ENGS = ("t", "v", "a", "g", "s")

    def __init__(self, nc, ctx):
        self.nc = nc
        self.q = {e: [] for e in self.ENGS}
        self.sems = {}
        self.waited = {e: {} for e in self.ENGS}

    def sem(self, ctx, name):
        h = ctx.enter_context(self.nc.semaphore(name))
        self.sems[name] = [h, 0]
        return name

    def cnt(self, name):
        return self.sems[name][1]

    def wait(self, eng, name, val):
        if val is None or val <= 0:
            return
        s = self.sems[name]
        assert val <= s[1], f"deadlock: wait {name}>={val} but only {s[1]} emitted"
        if self.waited[eng].get(name, 0) >= val:
            return
        self.waited[eng][name] = val
        self.q[eng].append(lambda e, h=s[0], v=val: e.wait_ge(h, v))

    def waits(self, eng, deps):
        for name, val in deps:
            self.wait(eng, name, val)

    def run(self, eng, fn, inc=None, n=1):
        if inc is not None:
            s = self.sems[inc]
            s[1] += n
            self.q[eng].append(lambda e, fn=fn, h=s[0], m=n: fn(e).then_inc(h, m))
            return s[1]
        self.q[eng].append(lambda e, fn=fn: fn(e))
        return None


def build_nc():
    nc = bass.Bass("TRN2", target_bir_lowering=False)

    xt = nc.dram_tensor("xt", [B * S, D], F32, kind="ExternalInput").ap()
    xf = nc.dram_tensor("xf", [B * D, S], F32, kind="ExternalInput").ap()
    w1 = nc.dram_tensor("w1", [D, D], F32, kind="ExternalInput").ap()
    w2 = nc.dram_tensor("w2", [D, D], F32, kind="ExternalInput").ap()
    w3 = nc.dram_tensor("w3", [D, D], F32, kind="ExternalInput").ap()
    w4 = nc.dram_tensor("w4", [D, D], F32, kind="ExternalInput").ap()
    wt = nc.dram_tensor("wt", [D, D], F32, kind="ExternalInput").ap()
    bqh = nc.dram_tensor("bqh", [128, 8], F32, kind="ExternalInput").ap()
    bkh = nc.dram_tensor("bkh", [128, 8], F32, kind="ExternalInput").ap()
    bvh = nc.dram_tensor("bvh", [128, 8], F32, kind="ExternalInput").ap()
    b4h = nc.dram_tensor("b4h", [128, D], F32, kind="ExternalInput").ap()
    idh = nc.dram_tensor("idh", [128, 128], BF16, kind="ExternalInput").ap()
    out = nc.dram_tensor("out", [B * S, D], F32, kind="ExternalOutput").ap()
    qts = nc.dram_tensor("qts", [B * 8 * 128, S], BF16, kind="Internal").ap()
    wbh = [nc.dram_tensor("wbh%d" % i, [D, D], BF16, kind="Internal").ap()
           for i in range(4)]
    xbh = nc.dram_tensor("xbh", [B * NTB * 128, 8 * TBW], BF16, kind="Internal").ap()

    ctx = ExitStack()
    with ctx:
        P = Prog(nc, ctx)
        for s_ in ("tt", "vv", "aa", "gg", "slc", "glw", "slw", "glx0", "glx1",
                   "slq0", "slq1", "slx", "slx2", "sxb", "gpc", "sq", "so"):
            P.sem(ctx, s_)

        def dma(eng, semname, fn, extra_waits=()):
            P.wait(eng, semname, P.cnt(semname))
            for nm, vl in extra_waits:
                P.wait(eng, nm, vl)
            return P.run(eng, fn, semname, 16)

        def sb(name, shape, dt=F32, c=ctx):
            return c.enter_context(nc.sbuf_tensor(name, shape, dt))

        KT = sb("KT", [128, 8, S], BF16)
        VP = sb("VP", [128, NST, D], BF16)
        wqT = sb("wqT", [128, 8, D], BF16)
        bq = sb("bq", [128, 8]); bk = sb("bk", [128, 8]); bv = sb("bv", [128, 8])
        b4b = sb("b4b", [128, D])
        ident = sb("ident", [128, 128], BF16)
        ssel = sb("ssel", [128, 1])
        ssm1 = sb("ssm1", [128, 1])
        epsln = sb("epsln", [128, 1])
        r127 = sb("r127", [128, 1])
        c1ap = sb("c1ap", [128, 1])
        nc1ap = sb("nc1ap", [128, 1])
        wscv = sb("wscv", [128, 1])
        stats = sb("stats", [128, 128])
        onesf = sb("onesf", [1, 128])

        for dst, src in ((bq, bqh), (bk, bkh), (bv, bvh), (b4b, b4h), (ident, idh)):
            dma("s", "slc", lambda e, d=dst, s_=src: e.dma_start(out=d[:], in_=s_))
        sl_static = P.cnt("slc")

        with ExitStack() as p0:
            wtf = sb("wtf", [128, 8, D], F32, p0)
            u0p = sb("u0p", [128, D], F32, p0)
            u1p = sb("u1p", [128, D], F32, p0)
            u2p = sb("u2p", [128, D], F32, p0)
            ps0 = p0.enter_context(nc.psum_tensor("ps0", [128, 1], F32))

            dma("s", "slc", lambda e: e.dma_start(
                out=wtf[:], in_=wt.rearrange("(c p) o -> p c o", p=128)))
            sl_wtf = P.cnt("slc")

            for wi, src in enumerate((w1, w2)):
                dma("g", "glw", lambda e, wi=wi, src=src: e.dma_start(
                    out=wbh[wi], in_=src))

            P.run("v", lambda e: e.memset(epsln[:], EPS_LN), "vv")
            P.run("v", lambda e: e.memset(c1ap[:], C1), "vv")
            P.run("v", lambda e: e.memset(nc1ap[:], -C1), "vv")
            v = P.run("v", lambda e: e.memset(stats[:, 122:123], 127.0), "vv")
            P.wait("v", "vv", v)
            v = P.run("v", lambda e: e.reciprocal(out=r127[:], in_=stats[:, 122:123]), "vv")
            P.wait("v", "vv", v)
            v = P.run("v", lambda e: e.memset(stats[:, 120:121], 1.25), "vv")
            P.wait("v", "vv", v)
            v = P.run("v", lambda e: e.tensor_scalar(
                out=stats[:, 121:122], in0=stats[:, 120:121], scalar1=C2,
                scalar2=None, op0=ADD), "vv")
            P.wait("v", "vv", v)
            v = P.run("v", lambda e: e.tensor_scalar(
                out=ssel[:], in0=stats[:, 121:122], scalar1=-C2,
                scalar2=None, op0=ADD), "vv")
            vv_ssel = v
            P.wait("v", "vv", v)
            v = P.run("v", lambda e: e.tensor_scalar(
                out=ssm1[:], in0=ssel[:], scalar1=-1.0, scalar2=1.0,
                op0=MUL, op1=ADD), "vv")

            P.wait("v", "slc", sl_wtf)
            for c in range(8):
                P.run("v", lambda e, c=c: e.tensor_reduce(
                    out=stats[:, 96 + c:97 + c], in_=wtf[:, c, :], axis=AX,
                    op=ADD, apply_absolute_value=True), "vv")
            v = P.cnt("vv")
            P.wait("v", "vv", v)
            v = P.run("v", lambda e: e.tensor_reduce(
                out=stats[:, 104:105], in_=stats[:, 96:104], axis=AX, op=ADD), "vv")
            P.wait("g", "vv", v)
            g = P.run("g", lambda e: e.tensor_reduce(
                out=stats[0:1, 105:106], in_=stats[:, 104:105], axis=AXC, op=ADD), "gg")
            for wi, src in enumerate((w3, w4)):
                dma("g", "glw", lambda e, wi=wi, src=src: e.dma_start(
                    out=wbh[wi + 2], in_=src))
            v = P.run("v", lambda e: e.memset(onesf[:], 1.0), "vv")
            P.wait("t", "vv", v)
            P.wait("t", "gg", g)
            t = P.run("t", lambda e: e.matmul(
                ps0[:], onesf[:], stats[0:1, 105:106], start=True, stop=True), "tt")
            P.wait("v", "tt", t)
            v = P.run("v", lambda e: e.tensor_copy(out=stats[:, 106:107], in_=ps0[:]), "vv")
            P.wait("v", "vv", v)
            v = P.run("v", lambda e: e.tensor_scalar(
                out=wscv[:], in0=stats[:, 106:107], scalar1=INVN, scalar2=EPS_Q,
                op0=MUL, op1=ADD), "vv")
            P.wait("v", "vv", v)
            v = P.run("v", lambda e: e.reciprocal(out=stats[:, 107:108], in_=wscv[:]), "vv")
            vv_rws = v

            for c in range(8):
                P.wait("v", "vv", P.cnt("vv"))
                v = P.run("v", lambda e, c=c: e.tensor_scalar(
                    out=u0p[:], in0=wtf[:, c, :], scalar1=stats[:, 107:108],
                    scalar2=None, op0=MUL), "vv")
                P.wait("v", "vv", v)
                v = P.run("v", lambda e: e.tensor_scalar(
                    out=u1p[:], in0=u0p[:], scalar1=C1, scalar2=None, op0=ADD), "vv")
                P.wait("v", "vv", v)
                v = P.run("v", lambda e: e.tensor_scalar(
                    out=u2p[:], in0=u1p[:], scalar1=-C1, scalar2=None, op0=ADD), "vv")
                P.wait("v", "vv", v)
                v = P.run("v", lambda e: e.tensor_scalar(
                    out=u1p[:], in0=u2p[:], scalar1=C2, scalar2=None, op0=ADD), "vv")
                P.wait("v", "vv", v)
                v = P.run("v", lambda e: e.scalar_tensor_tensor(
                    out=u1p[:], in0=u1p[:], scalar=-C2, op0=ADD,
                    in1=u2p[:], op1=SUB), "vv")
                P.wait("v", "vv", v)
                v = P.run("v", lambda e: e.scalar_tensor_tensor(
                    out=u0p[:], in0=u1p[:], scalar=ssel[:], op0=MUL,
                    in1=u2p[:], op1=ADD), "vv")
                P.wait("v", "vv", v)
                v = P.run("v", lambda e, c=c: e.tensor_scalar(
                    out=wqT[:, c, :], in0=u0p[:], scalar1=1.0, scalar2=-1.0,
                    op0=MINO, op1=MAXO), "vv")
            vv_wq = P.cnt("vv")

        def barrier():
            marks = {n: P.cnt(n) for n in P.sems}
            for e in Prog.ENGS:
                for n, v_ in marks.items():
                    P.wait(e, n, v_)
        barrier()

        kt_ready = [0, 0]
        vp_ready = [0, 0]
        qt_stored = [[0] * NTB, [0] * NTB]
        xbh_ready = {}

        phase2_end_tt = 0

        for b in range(B):
            with ExitStack() as p1:
                W1b = sb("W1b%d" % b, [128, 8, D], BF16, p1)
                W2b = sb("W2b%d" % b, [128, 8, D], BF16, p1)
                xtb = [sb(f"xtb{b}_{i}", [128, 8, TBW], BF16, p1) for i in range(2)]
                QTs = sb("QTs%d" % b, [128, 8, TBW], BF16, p1)
                psQ = p1.enter_context(nc.psum_tensor("psQ%d" % b, [128, 4, TBW], F32))
                psK = p1.enter_context(nc.psum_tensor("psK%d" % b, [128, 4, TBW], F32))

                xtb_free = [0, 0]
                bank_free_q = [0] * 4
                bank_free_k = [0] * 4
                qts_free = 0
                gl_xt = [0, 0]
                sl_xt = [0, 0]
                sxb_buf = [0, 0]

                def load_xk(k):
                    if (b, k) in xbh_ready:
                        row = (b * NTB + k) * 128
                        gl_xt[k % 2] = dma(
                            "g", "glx%d" % (k % 2),
                            lambda e, row=row, k=k: e.dma_start(
                                out=xtb[k % 2][:, 0:4, :],
                                in_=xbh[row:row + 128, 0:4 * TBW]
                                .rearrange("p (c t) -> p c t", c=4)),
                            extra_waits=(("tt", xtb_free[k % 2]),
                                         xbh_ready[(b, k)]))
                        sl_xt[k % 2] = dma(
                            "s", "slx2",
                            lambda e, row=row, k=k: e.dma_start(
                                out=xtb[k % 2][:, 4:8, :],
                                in_=xbh[row:row + 128, 4 * TBW:8 * TBW]
                                .rearrange("p (c t) -> p c t", c=4)),
                            extra_waits=(("tt", xtb_free[k % 2]),
                                         xbh_ready[(b, k)]))
                    else:
                        gl_xt[k % 2] = dma(
                            "g", "glx%d" % (k % 2),
                            lambda e, b=b, k=k: e.dma_start(
                                out=xtb[k % 2][:],
                                in_=xf[b * D:(b + 1) * D, k * TBW:(k + 1) * TBW]
                                .rearrange("(c p) t -> p c t", p=128)),
                            extra_waits=(("tt", xtb_free[k % 2]),
                                         ("sxb", sxb_buf[k % 2])))
                        sl_xt[k % 2] = 0
                        row = (b * NTB + k) * 128
                        sxb_buf[k % 2] = dma(
                            "s", "sxb",
                            lambda e, row=row, k=k: e.dma_start(
                                out=xbh[row:row + 128, :]
                                .rearrange("p (c t) -> p c t", c=8),
                                in_=xtb[k % 2][:]),
                            extra_waits=(("glx%d" % (k % 2), gl_xt[k % 2]),))
                        xbh_ready[(b, k)] = ("sxb", sxb_buf[k % 2])

                dma("g", "glw", lambda e: e.dma_start(
                    out=W1b[:, 0:4, :],
                    in_=wbh[0][0:D // 2, :].rearrange("(c p) o -> p c o", p=128)))
                gl_w1 = P.cnt("glw")
                dma("s", "slw", lambda e: e.dma_start(
                    out=W1b[:, 4:8, :],
                    in_=wbh[0][D // 2:D, :].rearrange("(c p) o -> p c o", p=128)))
                sl_w1 = P.cnt("slw")
                load_xk(0)
                dma("g", "glw", lambda e: e.dma_start(
                    out=W2b[:, 0:4, :],
                    in_=wbh[1][0:D // 2, :].rearrange("(c p) o -> p c o", p=128)))
                gl_w2 = P.cnt("glw")
                dma("s", "slw", lambda e: e.dma_start(
                    out=W2b[:, 4:8, :],
                    in_=wbh[1][D // 2:D, :].rearrange("(c p) o -> p c o", p=128)))
                sl_w2 = P.cnt("slw")
                if NTB > 1:
                    load_xk(1)

                for k in range(NTB):
                    evq = []
                    for oc in range(8):
                        P.wait("t", "glw", gl_w1)
                        P.wait("t", "slw", sl_w1)
                        P.wait("t", "glx%d" % (k % 2), gl_xt[k % 2])
                        P.wait("t", "slx2", sl_xt[k % 2])
                        P.wait("t", "vv", bank_free_q[oc % 4])
                        for e_ in range(8):
                            tv = P.run("t", lambda e, oc=oc, e_=e_, k=k: e.matmul(
                                psQ[:, oc % 4, :],
                                W1b[:, e_, oc * 128:(oc + 1) * 128],
                                xtb[k % 2][:, e_, :],
                                start=(e_ == 0), stop=(e_ == 7)),
                                "tt" if e_ == 7 else None)
                        P.wait("v", "tt", tv)
                        P.wait("v", "sq", qts_free)
                        ev = P.run("v", lambda e, oc=oc: e.tensor_scalar(
                            out=QTs[:, oc, :], in0=psQ[:, oc % 4, :],
                            scalar1=bq[:, oc:oc + 1], scalar2=None, op0=ADD), "vv")
                        bank_free_q[oc % 4] = ev
                        evq.append(ev)
                    qt_stored[b][k] = dma(
                        "s", "sq",
                        lambda e, b=b, k=k: e.dma_start(
                            out=qts[b * 1024:(b + 1) * 1024, k * TBW:(k + 1) * TBW]
                            .rearrange("(c p) t -> p c t", p=128),
                            in_=QTs[:]),
                        extra_waits=(("vv", evq[-1]),))
                    qts_free = qt_stored[b][k]

                    for oc in range(8):
                        P.wait("t", "glw", gl_w2)
                        P.wait("t", "slw", sl_w2)
                        P.wait("t", "vv", bank_free_k[oc % 4])
                        for e_ in range(8):
                            tv = P.run("t", lambda e, oc=oc, e_=e_, k=k: e.matmul(
                                psK[:, oc % 4, :],
                                W2b[:, e_, oc * 128:(oc + 1) * 128],
                                xtb[k % 2][:, e_, :],
                                start=(e_ == 0), stop=(e_ == 7)),
                                "tt" if e_ == 7 else None)
                        P.wait("v", "tt", tv)
                        ev = P.run("v", lambda e, oc=oc, k=k: e.tensor_scalar(
                            out=KT[:, oc, k * TBW:(k + 1) * TBW], in0=psK[:, oc % 4, :],
                            scalar1=bk[:, oc:oc + 1], scalar2=None, op0=ADD), "vv")
                        bank_free_k[oc % 4] = ev
                    xtb_free[k % 2] = tv
                    if k + 2 < NTB:
                        load_xk(k + 2)
                kt_ready[b] = P.cnt("vv")
            barrier()

            with ExitStack() as p1v:
                W3b = sb("W3b%d" % b, [128, 8, D], BF16, p1v)
                W4b = sb("W4b%d" % b, [128, 8, D], BF16, p1v)
                xtb = [sb(f"xvb{b}_{i}", [128, 8, TBW], BF16, p1v) for i in range(2)]
                VTb = sb("VTb%d" % b, [128, 8, TBW], BF16, p1v)
                psV = p1v.enter_context(nc.psum_tensor("psV%d" % b, [128, 4, TBW], F32))
                psVp = p1v.enter_context(nc.psum_tensor("psVp%d" % b, [128, 2, TBW], F32))

                xtb_free = [0, 0]
                bank_free_v = [0] * 4
                bank_free_p = [0, 0]
                vtb_free = 0
                gl_xt = [0, 0]
                sl_xt = [0, 0]

                def load_xk(k):
                    row = (b * NTB + k) * 128
                    gl_xt[k % 2] = dma(
                        "g", "glx%d" % (k % 2),
                        lambda e, row=row, k=k: e.dma_start(
                            out=xtb[k % 2][:, 0:4, :],
                            in_=xbh[row:row + 128, 0:4 * TBW]
                            .rearrange("p (c t) -> p c t", c=4)),
                        extra_waits=(("tt", xtb_free[k % 2]),
                                     xbh_ready[(b, k)]))
                    sl_xt[k % 2] = dma(
                        "s", "slx2",
                        lambda e, row=row, k=k: e.dma_start(
                            out=xtb[k % 2][:, 4:8, :],
                            in_=xbh[row:row + 128, 4 * TBW:8 * TBW]
                            .rearrange("p (c t) -> p c t", c=4)),
                        extra_waits=(("tt", xtb_free[k % 2]),
                                     xbh_ready[(b, k)]))

                dma("g", "glw", lambda e: e.dma_start(
                    out=W3b[:, 0:4, :],
                    in_=wbh[2][0:D // 2, :].rearrange("(c p) o -> p c o", p=128)))
                gl_w3 = P.cnt("glw")
                dma("s", "slw", lambda e: e.dma_start(
                    out=W3b[:, 4:8, :],
                    in_=wbh[2][D // 2:D, :].rearrange("(c p) o -> p c o", p=128)))
                sl_w3 = P.cnt("slw")
                load_xk(0)
                dma("g", "glw", lambda e: e.dma_start(
                    out=W4b[:, 0:4, :],
                    in_=wbh[3][0:D // 2, :].rearrange("(c p) o -> p c o", p=128)))
                gl_w4 = P.cnt("glw")
                dma("s", "slw", lambda e: e.dma_start(
                    out=W4b[:, 4:8, :],
                    in_=wbh[3][D // 2:D, :].rearrange("(c p) o -> p c o", p=128)))
                sl_w4 = P.cnt("slw")
                if NTB > 1:
                    load_xk(1)

                for k in range(NTB):
                    vt_evacs = []
                    for oc in range(8):
                        P.wait("t", "glw", gl_w3)
                        P.wait("t", "slw", sl_w3)
                        P.wait("t", "glx%d" % (k % 2), gl_xt[k % 2])
                        P.wait("t", "slx2", sl_xt[k % 2])
                        P.wait("t", "vv", bank_free_v[oc % 4])
                        for e_ in range(8):
                            tv = P.run("t", lambda e, oc=oc, e_=e_, k=k: e.matmul(
                                psV[:, oc % 4, :],
                                W3b[:, e_, oc * 128:(oc + 1) * 128],
                                xtb[k % 2][:, e_, :],
                                start=(e_ == 0), stop=(e_ == 7)),
                                "tt" if e_ == 7 else None)
                        P.wait("v", "tt", tv)
                        P.wait("v", "tt", vtb_free)
                        ev = P.run("v", lambda e, oc=oc: e.tensor_scalar(
                            out=VTb[:, oc, :], in0=psV[:, oc % 4, :],
                            scalar1=bv[:, oc:oc + 1], scalar2=None, op0=ADD), "vv")
                        bank_free_v[oc % 4] = ev
                        vt_evacs.append(ev)
                    xtb_free[k % 2] = tv

                    for tt_ in range(4):
                        for h in range(2):
                            P.wait("t", "glw", gl_w4)
                            P.wait("t", "slw", sl_w4)
                            P.wait("t", "vv", vt_evacs[-1])
                            P.wait("t", "vv", bank_free_p[h])
                            for e_ in range(8):
                                tv = P.run("t", lambda e, tt_=tt_, h=h, e_=e_: e.matmul(
                                    psVp[:, h, :],
                                    VTb[:, e_, tt_ * 128:(tt_ + 1) * 128],
                                    W4b[:, e_, h * TBW:(h + 1) * TBW],
                                    start=(e_ == 0), stop=(e_ == 7)),
                                    "tt" if e_ == 7 else None)
                            P.wait("v", "tt", tv)
                            ev = P.run("v", lambda e, k=k, tt_=tt_, h=h: e.tensor_tensor(
                                out=VP[:, k * 4 + tt_, h * TBW:(h + 1) * TBW],
                                in0=psVp[:, h, :],
                                in1=b4b[:, h * TBW:(h + 1) * TBW], op=ADD), "vv")
                            bank_free_p[h] = ev
                    vtb_free = tv
                    if k + 2 < NTB:
                        load_xk(k + 2)
                vp_ready[b] = P.cnt("vv")
            barrier()

            with ExitStack() as p2:
                scor = sb("scor%d" % b, [128, S], F32, p2)
                Pb = sb("Pb%d" % b, [128, S], BF16, p2)
                PT = sb("PT%d" % b, [128, NST, 128], BF16, p2)
                qtt = sb("qtt%d" % b, [128, 8, 128], BF16, p2)
                attn = sb("attn%d" % b, [128, D], F32, p2)
                x1bufs = [sb(f"x1_{b}_{i}", [128, D], F32, p2) for i in range(2)]
                r2b = sb("r2b%d" % b, [128, D], F32, p2)
                if S >= 3072 + 1024:
                    outbuf = scor[:, 0:1024]
                    u2 = scor[:, 2048:3072]
                    u3 = scor[:, 3072:4096]
                else:
                    outbuf = sb("outb%d" % b, [128, D], F32, p2)
                    u2 = sb("u2b%d" % b, [128, D], F32, p2)
                    u3 = sb("u3b%d" % b, [128, D], F32, p2)
                aqbufs = [sb(f"aqb{b}_{i}", [128, D], BF16, p2) for i in range(2)]
                aqT = sb("aqT%d" % b, [128, 8, 128], BF16, p2)
                psS = p2.enter_context(nc.psum_tensor("psS%d" % b, [128, 4, TBW], F32))
                psT0 = p2.enter_context(nc.psum_tensor("psT0%d" % b, [128, GSZ, 128], BF16))
                psT1 = p2.enter_context(nc.psum_tensor("psT1%d" % b, [128, GSZ, 128], BF16))
                psO = p2.enter_context(nc.psum_tensor("psO%d" % b, [128, 2, TBW], F32))
                psTs = [psT0, psT1]

                sc_stop = {}
                sc_evac_all = {}
                sc_last = {}
                pt_evac = {}
                pt_stop = {}
                pt_last = {-1: 0, -2: 0}
                pv_stop = {}
                pv_q = {}
                attn_evac = {}
                exp_done = {-1: 0}
                rrs_done = {}
                x1_done = {-1: 0}
                aq_done = {-1: 0}
                radd_done = {}
                rfuse = {}
                zc_a_done = {}
                r2h_done = {-1: 0, -2: 0}
                bn2_done = {-1: 0, -2: 0}
                out_done = {-1: 0, -2: 0}
                so_done = {-1: 0, -2: 0}
                ffn_stop = {}
                aqtr_done = {-1: 0}
                aqT_evac = {-1: 0}
                qt_ld = {}
                x_ld = {}
                ysc_done = {-1: 0, -2: 0}
                bn1_done = {}
                psT_free = {0: 0, 1: 0}
                stats_free = {0: (0, 0), 1: (0, 0)}

                def qt_load(i):
                    qt_ld[i] = dma(
                        "g", "slq0",
                        lambda e, b=b, i=i: e.dma_start(
                            out=qtt[:],
                            in_=qts[b * 1024:(b + 1) * 1024, i * 128:(i + 1) * 128]
                            .rearrange("(c p) t -> p c t", p=128)),
                        extra_waits=(("sq", qt_stored[b][NTB - 1]),
                                     ("tt", sc_last.get(i - 1, 0))))

                qt_load(0)

                for it in range(NST + 3):
                    i = it
                    ie = it - 1
                    if_ = it - 3
                    p = i % 2
                    pe = ie % 2
                    sbase = 64 * p
                    sbe = 64 * pe
                    sbf = 64 * (if_ % 2)
                    x1e = x1bufs[ie % 2]
                    x1f = x1bufs[if_ % 2]
                    aqe = aqbufs[ie % 2]
                    aqf = aqbufs[if_ % 2]

                    def sc_block(j):
                        P.wait("t", "slq0", qt_ld[i])
                        P.wait("t", "vv", kt_ready[b])
                        mb0 = sbase + 0
                        pg = i * NSB + j - 4
                        if pg >= 0:
                            P.wait("t", "vv",
                                   sc_evac_all.get((pg // NSB, pg % NSB), 0))
                        for e_ in range(8):
                            tv = P.run("t", lambda e, i=i, j=j, e_=e_: e.matmul(
                                psS[:, j % 4, :],
                                qtt[:, e_, :],
                                KT[:, e_, j * TBW:(j + 1) * TBW],
                                start=(e_ == 0), stop=(e_ == 7)),
                                "tt" if e_ == 7 else None)
                        sc_stop[(i, j)] = tv
                        if j == 0:
                            P.wait("v", "aa", exp_done.get(i - 1, 0))
                            P.wait("v", "vv", out_done.get(i - 4, 0))
                            P.wait("v", "so", so_done.get(i - 4, 0))
                            P.wait("v", "vv", aq_done.get(i - 2, 0))
                            P.wait("v", "vv", rfuse.get((i - 2, 1), 0))
                        if j % 2 == 1 or NSB == 1:
                            P.wait("v", "tt", tv)
                            jp = j // 2
                            j0_, wid = (j - 1, 2) if NSB > 1 else (j, 1)
                            ev = P.run("v", lambda e, j0_=j0_, wid=wid, jp=jp, mb0=mb0: e.tensor_scalar(
                                out=scor[:, j0_ * TBW:(j0_ + wid) * TBW],
                                in0=psS[:, j0_ % 4:j0_ % 4 + wid, :],
                                scalar1=1.0, scalar2=NEG_BIG, op0=MUL, op1=MAXO,
                                accum_out=stats[:, mb0 + jp:mb0 + jp + 1]), "vv")
                            sc_evac_all[(i, j0_)] = ev
                            sc_evac_all[(i, j)] = ev

                    if i < NST:
                        for j in range(NSB // 2):
                            sc_block(j)

                    iq = it - 2
                    if 0 <= iq < NST:
                        sbq = 64 * (iq % 2)
                        x1q = x1bufs[iq % 2]
                        aqq = aqbufs[iq % 2]
                        P.wait("v", "vv", bn1_done[iq])
                        b6 = sbq + 12
                        v = P.run("v", lambda e, b6=b6, sbq=sbq: e.bn_aggr(
                            out=stats[:, sbq + 24:sbq + 26], in_=stats[:, b6:b6 + 12]), "vv")
                        P.wait("a", "vv", v)
                        a = P.run("a", lambda e, sbq=sbq: e.activation(
                            out=stats[:, sbq + 27:sbq + 28], in_=stats[:, sbq + 25:sbq + 26],
                            func=AF.Ln, bias=epsln[:], scale=1.0), "aa")
                        P.wait("a", "aa", a)
                        a = P.run("a", lambda e, sbq=sbq: e.activation(
                            out=stats[:, sbq + 28:sbq + 29], in_=stats[:, sbq + 27:sbq + 28],
                            func=AF.Exp, bias=0.0, scale=-0.5), "aa")
                        rstd_a = a
                        P.wait("v", "aa", rstd_a)
                        P.wait("v", "vv", ysc_done.get(iq - 2, 0))
                        P.run("v", lambda e, sbq=sbq: e.tensor_reduce(
                            out=stats[:, sbq + 58:sbq + 59], in_=stats[:, sbq + 54:sbq + 56],
                            axis=AX, op=MAXO), "vv")
                        v = P.run("v", lambda e, sbq=sbq: e.tensor_reduce(
                            out=stats[:, sbq + 59:sbq + 60], in_=stats[:, sbq + 56:sbq + 58],
                            axis=AX, op=MINO), "vv")
                        P.wait("v", "vv", v)
                        v = P.run("v", lambda e, sbq=sbq: e.tensor_scalar(
                            out=stats[:, sbq + 60:sbq + 62], in0=stats[:, sbq + 58:sbq + 60],
                            scalar1=stats[:, sbq + 24:sbq + 25], scalar2=None,
                            op0=SUB), "vv")
                        P.wait("v", "vv", v)
                        v = P.run("v", lambda e, sbq=sbq: e.tensor_reduce(
                            out=stats[:, sbq + 62:sbq + 63], in_=stats[:, sbq + 60:sbq + 62],
                            axis=AX, op=MAXO, apply_absolute_value=True), "vv")
                        P.wait("v", "vv", v)
                        v = P.run("v", lambda e, sbq=sbq: e.tensor_scalar(
                            out=stats[:, sbq + 31:sbq + 32], in0=stats[:, sbq + 62:sbq + 63],
                            scalar1=stats[:, sbq + 28:sbq + 29], scalar2=EPS_Q,
                            op0=MUL, op1=ADD), "vv")
                        P.wait("v", "vv", v)
                        v = P.run("v", lambda e, sbq=sbq: e.reciprocal(
                            out=stats[:, sbq + 32:sbq + 33], in_=stats[:, sbq + 31:sbq + 32]), "vv")
                        P.wait("v", "vv", v)
                        v = P.run("v", lambda e, sbq=sbq: e.tensor_scalar(
                            out=stats[:, sbq + 33:sbq + 34], in0=stats[:, sbq + 32:sbq + 33],
                            scalar1=127.0, scalar2=stats[:, sbq + 28:sbq + 29],
                            op0=MUL, op1=MUL), "vv")
                        zq_v = v
                        P.wait("v", "vv", zq_v)
                        v = P.run("v", lambda e, sbq=sbq: e.scalar_tensor_tensor(
                            out=stats[:, sbq + 63:sbq + 64], in0=stats[:, sbq + 24:sbq + 25],
                            scalar=-1.0, op0=MUL, in1=stats[:, sbq + 33:sbq + 34],
                            op1=MUL), "vv")
                        nmq_v = v
                        v = P.run("v", lambda e, sbq=sbq: e.scalar_tensor_tensor(
                            out=stats[:, sbq + 29:sbq + 30], in0=stats[:, sbq + 24:sbq + 25],
                            scalar=-1.0, op0=MUL, in1=stats[:, sbq + 28:sbq + 29],
                            op1=MUL), "vv")
                        nmr_v = v
                        P.wait("a", "vv", nmr_v)
                        P.wait("a", "vv", out_done.get(iq - 2, 0))
                        a = P.run("a", lambda e, sbq=sbq, x1q=x1q: e.activation(
                            out=x1q[:], in_=attn[:], func=AF.Identity,
                            scale=stats[:, sbq + 28:sbq + 29],
                            bias=stats[:, sbq + 29:sbq + 30]), "aa")
                        x1_done[iq] = a

                    if if_ >= 0:
                        for h in range(2):
                            P.wait("t", "aa", aqT_evac[if_])
                            P.wait("t", "vv", vv_wq)
                            P.wait("t", "vv", r2h_done.get(if_ - 1, 0))
                            P.wait("t", "vv", rfuse.get((ie - 1, h), 0))
                            for ic in range(8):
                                tv = P.run("t", lambda e, ic=ic, h=h: e.matmul(
                                    psO[:, h, :], aqT[:, ic, :],
                                    wqT[:, ic, h * TBW:(h + 1) * TBW],
                                    start=(ic == 0), stop=(ic == 7)),
                                    "tt" if ic == 7 else None)
                            ffn_stop[(if_, h)] = tv
                        v = P.run("v", lambda e, sbf=sbf: e.tensor_scalar(
                            out=stats[:, sbf + 34:sbf + 35], in0=stats[:, sbf + 31:sbf + 32],
                            scalar1=wscv[:], scalar2=r127[:], op0=MUL, op1=MUL), "vv")
                        ysc_done[if_] = v
                        P.wait("v", "tt", ffn_stop[(if_, 1)])
                        P.wait("v", "vv", v)
                        v2 = P.run("v", lambda e, sbf=sbf, x1f=x1f: e.scalar_tensor_tensor(
                            out=x1f[:], in0=psO[:, 0:2, :],
                            scalar=stats[:, sbf + 34:sbf + 35], op0=MUL,
                            in1=x1f[:], op1=ADD), "vv")
                        r2h_done[if_] = v2

                    if i < NST:
                        for j in range(NSB // 2, NSB):
                            sc_block(j)

                    if i < NST:
                        sc_last[i] = sc_stop[(i, NSB - 1)]
                        P.wait("v", "vv", sc_evac_all[(i, NSB - 1)])
                        v = P.run("v", lambda e, sbase=sbase: e.tensor_reduce(
                            out=stats[:, sbase + 8:sbase + 9],
                            in_=stats[:, sbase:sbase + NSB],
                            axis=AX, op=MAXO), "vv")
                        P.wait("v", "vv", v)
                        v = P.run("v", lambda e, sbase=sbase: e.tensor_scalar(
                            out=stats[:, sbase + 9:sbase + 10],
                            in0=stats[:, sbase + 8:sbase + 9],
                            scalar1=-INV32, scalar2=None, op0=MUL), "vv")
                        negm_v = v
                        P.wait("a", "vv", negm_v)
                        P.wait("a", "tt", pt_last.get(i - 1, 0))
                        a = P.run("a", lambda e, sbase=sbase: e.activation(
                            out=Pb[:], in_=scor[:], func=AF.Exp, scale=INV32,
                            bias=stats[:, sbase + 9:sbase + 10],
                            accum_out=stats[:, sbase + 10:sbase + 11]), "aa")
                        exp_done[i] = a
                        P.wait("v", "aa", a)
                        v = P.run("v", lambda e, sbase=sbase: e.reciprocal(
                            out=stats[:, sbase + 11:sbase + 12],
                            in_=stats[:, sbase + 10:sbase + 11]), "vv")
                        rrs_done[i] = v
                        if i + 1 < NST:
                            qt_load(i + 1)

                    if 0 <= iq < NST:
                        P.wait("v", "vv", nmq_v)
                        P.wait("v", "aa", x1_done[iq])
                        v = P.run("v", lambda e, sbq=sbq, u2=u2: e.tensor_scalar(
                            out=u2[:], in0=attn[:], scalar1=stats[:, sbq + 33:sbq + 34],
                            scalar2=stats[:, sbq + 63:sbq + 64], op0=MUL, op1=ADD), "vv")
                        P.wait("v", "vv", v)
                        v = P.run("v", lambda e, u2=u2: e.tensor_scalar(
                            out=attn[:], in0=u2[:], scalar1=127.0, scalar2=-128.0,
                            op0=MINO, op1=MAXO), "vv")
                        P.wait("v", "vv", v)
                        v = P.run("v", lambda e, u2=u2: e.tensor_scalar(
                            out=u2[:], in0=attn[:], scalar1=C1, scalar2=None,
                            op0=ADD), "vv")
                        P.wait("v", "vv", v)
                        v = P.run("v", lambda e, u2=u2: e.tensor_scalar(
                            out=attn[:], in0=u2[:], scalar1=-C1, scalar2=None,
                            op0=ADD), "vv")
                        P.wait("v", "vv", v)
                        v = P.run("v", lambda e, u2=u2: e.tensor_scalar(
                            out=u2[:], in0=attn[:], scalar1=C2, scalar2=-C2,
                            op0=ADD, op1=ADD), "vv")
                        P.wait("v", "vv", v)
                        v = P.run("v", lambda e, u2=u2, u3=u3: e.tensor_scalar(
                            out=u3[:], in0=u2[:], scalar1=ssel[:], scalar2=None,
                            op0=MUL), "vv")
                        P.wait("v", "vv", v)
                        P.wait("v", "tt", aqtr_done.get(iq - 2, 0))
                        v = P.run("v", lambda e, aqq=aqq, u3=u3: e.scalar_tensor_tensor(
                            out=aqq[:], in0=attn[:], scalar=ssm1[:], op0=MUL,
                            in1=u3[:], op1=ADD), "vv")
                        aq_done[iq] = v

                    def pv_group(h, q):
                        if h == 0 and q == 0:
                            P.wait("t", "vv", vp_ready[b])
                            P.wait("t", "vv", rfuse.get((ie - 1, 0), 0))
                            P.wait("t", "vv", rfuse.get((ie - 1, 1), 0))
                            P.wait("t", "vv", r2h_done.get(if_, 0))
                        P.wait("t", "aa", pt_evac.get((ie, q), 0))
                        tv = None
                        for tc in range(q * GSZ, (q + 1) * GSZ):
                            tv = P.run("t", lambda e, tc=tc, h=h: e.matmul(
                                psO[:, h, :], PT[:, tc, :],
                                VP[:, tc, h * TBW:(h + 1) * TBW],
                                start=(tc == 0), stop=(tc == NST - 1),
                                skip_group_check=True),
                                "tt" if (tc == NST - 1 or tc == (q + 1) * GSZ - 1) else None)
                        if q == NG - 1:
                            pv_stop[(ie, h)] = tv
                        pv_q[(ie, h, q)] = tv

                    def attn_evac_h(h):
                        P.wait("v", "tt", pv_stop[(ie, h)])
                        P.wait("v", "vv", rrs_done[ie])
                        P.wait("v", "vv", aq_done.get(ie - 1, 0))
                        P.wait("v", "slx", x_ld[ie])
                        rf = P.run("v", lambda e, h=h, sbe=sbe: e.scalar_tensor_tensor(
                            out=attn[:, h * TBW:(h + 1) * TBW], in0=psO[:, h, :],
                            scalar=stats[:, sbe + 11:sbe + 12], op0=MUL,
                            in1=r2b[:, h * TBW:(h + 1) * TBW], op1=ADD), "vv")
                        rfuse[(ie, h)] = rf
                        P.wait("v", "vv", rf)
                        b6 = sbe + 12
                        P.run("v", lambda e, h=h, sbe=sbe: e.tensor_reduce(
                            out=stats[:, sbe + 54 + h:sbe + 55 + h],
                            in_=attn[:, h * TBW:(h + 1) * TBW], axis=AX, op=MAXO), "vv")
                        P.run("v", lambda e, h=h, sbe=sbe: e.tensor_reduce(
                            out=stats[:, sbe + 56 + h:sbe + 57 + h],
                            in_=attn[:, h * TBW:(h + 1) * TBW], axis=AX, op=MINO), "vv")
                        return P.run("v", lambda e, h=h, b6=b6: e.bn_stats(
                            out=stats[:, b6 + 6 * h:b6 + 6 * h + 6],
                            in_=attn[:, h * TBW:(h + 1) * TBW]), "vv")

                    def pt_group(g_):
                        P.wait("t", "aa", exp_done[i])
                        P.wait("t", "aa", psT_free[g_ % 2])
                        for kk in range(GSZ):
                            tv = P.run("t", lambda e, g_=g_, kk=kk: e.transpose(
                                psTs[g_ % 2][:, kk, :],
                                Pb[:, (g_ * GSZ + kk) * 128:(g_ * GSZ + kk + 1) * 128],
                                ident[:]), "tt" if kk == GSZ - 1 else None)
                        pt_stop[(i, g_)] = tv
                        P.wait("a", "tt", tv)
                        P.wait("a", "tt", pv_q.get((ie, 1, g_), 0))
                        ev = P.run("a", lambda e, g_=g_: e.activation(
                            out=PT[:, g_ * GSZ:(g_ + 1) * GSZ, :], in_=psTs[g_ % 2][:],
                            func=AF.Identity, bias=0.0, scale=1.0), "aa")
                        pt_evac[(i, g_)] = ev
                        psT_free[g_ % 2] = ev

                    if 0 <= ie and i < NST and NG >= 4:
                        for q in range(NG):
                            pv_group(0, q)
                        attn_evac_h(0)
                        pv_group(1, 0)
                        pt_group(0)
                        pv_group(1, 1)
                        pt_group(1)
                        pv_group(1, 2)
                        pt_group(2)
                        pv_group(1, 3)
                        pt_group(3)
                        bn1_done[ie] = attn_evac_h(1)
                    else:
                        if 0 <= ie < NST:
                            for h in range(2):
                                for q in range(NG):
                                    pv_group(h, q)
                            attn_evac_h(0)
                            bn1_done[ie] = attn_evac_h(1)
                        if i < NST:
                            for g_ in range(NG):
                                pt_group(g_)
                    if i < NST:
                        pt_last[i] = pt_stop[(i, NG - 1)]

                    ic_ = it - 2
                    if 0 <= ic_ < NST:
                        aqc = aqbufs[ic_ % 2]
                        P.wait("t", "vv", aq_done[ic_])
                        for ga in range((8 + GSZ - 1) // GSZ):
                            lo, hi = ga * GSZ, min((ga + 1) * GSZ, 8)
                            P.wait("t", "aa", psT_free[0])
                            for kk in range(lo, hi):
                                tv = P.run("t", lambda e, kk=kk, lo=lo, aqc=aqc: e.transpose(
                                    psT0[:, kk - lo, :], aqc[:, kk * 128:(kk + 1) * 128],
                                    ident[:]), "tt" if kk == hi - 1 else None)
                            P.wait("a", "tt", tv)
                            P.wait("a", "tt", ffn_stop.get((ic_ - 1, 1), 0))
                            ev = P.run("a", lambda e, lo=lo, hi=hi: e.activation(
                                out=aqT[:, lo:hi, :], in_=psT0[:, 0:hi - lo, :],
                                func=AF.Identity, bias=0.0, scale=1.0), "aa")
                            psT_free[0] = ev
                        aqtr_done[ic_] = tv
                        aqT_evac[ic_] = ev

                    if if_ >= 0:
                        P.wait("v", "vv", r2h_done[if_])
                        b62 = sbf + 36
                        P.run("v", lambda e, b62=b62, x1f=x1f: e.bn_stats(
                            out=stats[:, b62:b62 + 6], in_=x1f[:, 0:512]), "vv")
                        v = P.run("v", lambda e, b62=b62, x1f=x1f: e.bn_stats(
                            out=stats[:, b62 + 6:b62 + 12], in_=x1f[:, 512:1024]), "vv")
                        P.wait("v", "vv", v)
                        v = P.run("v", lambda e, b62=b62, sbf=sbf: e.bn_aggr(
                            out=stats[:, sbf + 48:sbf + 50], in_=stats[:, b62:b62 + 12]), "vv")
                        bn2_done[if_] = v
                        P.wait("a", "vv", v)
                        a = P.run("a", lambda e, sbf=sbf: e.activation(
                            out=stats[:, sbf + 51:sbf + 52], in_=stats[:, sbf + 49:sbf + 50],
                            func=AF.Ln, bias=epsln[:], scale=1.0), "aa")
                        P.wait("a", "aa", a)
                        a = P.run("a", lambda e, sbf=sbf: e.activation(
                            out=stats[:, sbf + 52:sbf + 53], in_=stats[:, sbf + 51:sbf + 52],
                            func=AF.Exp, bias=0.0, scale=-0.5), "aa")
                        P.wait("v", "aa", a)
                        v = P.run("v", lambda e, sbf=sbf: e.scalar_tensor_tensor(
                            out=stats[:, sbf + 53:sbf + 54], in0=stats[:, sbf + 48:sbf + 49],
                            scalar=-1.0, op0=MUL, in1=stats[:, sbf + 52:sbf + 53],
                            op1=MUL), "vv")
                        P.wait("v", "vv", v)
                        P.wait("v", "so", so_done.get(if_ - 1, 0))
                        v = P.run("v", lambda e, sbf=sbf, x1f=x1f: e.tensor_scalar(
                            out=outbuf[:], in0=x1f[:],
                            scalar1=stats[:, sbf + 52:sbf + 53],
                            scalar2=stats[:, sbf + 53:sbf + 54],
                            op0=MUL, op1=ADD), "vv")
                        out_done[if_] = v
                        so_done[if_] = dma(
                            "s", "so",
                            lambda e, b=b, if_=if_: e.dma_start(
                                out=out[b * S + if_ * 128:b * S + (if_ + 1) * 128, :],
                                in_=outbuf[:]),
                            extra_waits=(("vv", v),))

                    if i < NST:
                        x_ld[i] = dma(
                            "g", "slx",
                            lambda e, b=b, i=i: e.dma_start(
                                out=r2b[:],
                                in_=xt[b * S + i * 128:b * S + (i + 1) * 128, :]),
                            extra_waits=(("vv", rfuse.get((i - 1, 1), 0)),))

                    if b + 1 < B and it % 3 == 1 and it // 3 < NTB:
                        kx = it // 3
                        row = ((b + 1) * NTB + kx) * 128
                        xbh_ready[(b + 1, kx)] = (
                            "gpc",
                            dma("g", "gpc",
                                lambda e, b=b, kx=kx, row=row: e.dma_start(
                                    out=xbh[row:row + 128, :]
                                    .rearrange("p (c t) -> p c t", c=8),
                                    in_=xf[(b + 1) * D:(b + 2) * D,
                                           kx * TBW:(kx + 1) * TBW]
                                    .rearrange("(c p) t -> p c t", p=128))))

                phase2_end_tt = P.cnt("tt")
            barrier()

        P.wait("s", "so", P.cnt("so"))

        with nc.Block() as block:
            @block.tensor
            def _(e):
                for f in P.q["t"]:
                    f(e)

            @block.vector
            def _(e):
                for f in P.q["v"]:
                    f(e)

            @block.scalar
            def _(e):
                for f in P.q["a"]:
                    f(e)

            @block.gpsimd
            def _(e):
                for f in P.q["g"]:
                    f(e)

            @block.sync
            def _(e):
                for f in P.q["s"]:
                    f(e)

    return nc


_NC_CACHE = None


def kernel(L0_attn_x, L0_attn_W1, L0_attn_b1, L0_attn_W2, L0_attn_b2,
           L0_attn_W3, L0_attn_b3, L0_attn_W4, L0_attn_b4, L0_ffn_weight):
    global _NC_CACHE
    x = np.ascontiguousarray(np.asarray(L0_attn_x, dtype=np.float32))
    W = [np.ascontiguousarray(np.asarray(w, dtype=np.float32))
         for w in (L0_attn_W1, L0_attn_W2, L0_attn_W3, L0_attn_W4)]
    bs = [np.ascontiguousarray(np.asarray(v, dtype=np.float32))
          for v in (L0_attn_b1, L0_attn_b2, L0_attn_b3, L0_attn_b4)]
    wf = np.ascontiguousarray(np.asarray(L0_ffn_weight, dtype=np.float32))
    wt = np.ascontiguousarray(wf.T)

    bqh = np.ascontiguousarray(bs[0].reshape(8, 128).T)
    bkh = np.ascontiguousarray(bs[1].reshape(8, 128).T)
    bvh = np.ascontiguousarray(bs[2].reshape(8, 128).T)
    b4h = np.ascontiguousarray(np.broadcast_to(bs[3], (128, D)))
    idh = np.eye(128, dtype=ml_dtypes.bfloat16)

    if _NC_CACHE is None:
        _NC_CACHE = build_nc()
    nc = _NC_CACHE

    in_maps = []
    for c in range(8):
        xs = x[2 * c:2 * c + 2]
        in_maps.append({
            "xt": np.ascontiguousarray(xs.reshape(B * S, D)),
            "xf": np.ascontiguousarray(xs.transpose(0, 2, 1).reshape(B * D, S)),
            "w1": W[0], "w2": W[1], "w3": W[2], "w4": W[3],
            "wt": wt, "bqh": bqh, "bkh": bkh, "bvh": bvh, "b4h": b4h,
            "idh": idh,
        })
    res = run_bass_kernel_spmd(nc, in_maps, core_ids=list(range(8)))
    outs = [np.asarray(res.results[c]["out"], dtype=np.float32).reshape(B, S, D)
            for c in range(8)]
    return np.concatenate(outs, axis=0)


# version 29
# speedup vs baseline: 1.1205x; 1.1205x over previous
import numpy as np
import ml_dtypes
from contextlib import ExitStack

import concourse.bass as bass
import concourse.mybir as mybir
from concourse.bass_utils import run_bass_kernel_spmd
from concourse.alu_op_type import AluOpType

F32 = mybir.dt.float32
BF16 = mybir.dt.bfloat16
AX = mybir.AxisListType.X
AXC = mybir.AxisListType.C
AF = mybir.ActivationFunctionType
ADD, SUB, MUL = AluOpType.add, AluOpType.subtract, AluOpType.mult
MAXO, MINO = AluOpType.max, AluOpType.min

C1 = float(3 << 22)
C2 = float(3 << 51)
NEG_BIG = -3.0e38
INV32 = 0.03125
INV127 = float(np.float32(np.float64(1.0) / np.float64(127.0)))
INVN = float(np.float32(1.0) / np.float32(1048576.0))
EPS_LN = 1e-5
EPS_Q = 1e-8

B, S, D = 2, 4096, 1024
NTB, TBW = 8, 512
NST = 32
NSB = 8
GSZ = 8
NG = 4


class Prog:
    ENGS = ("t", "v", "a", "g", "s")

    def __init__(self, nc, ctx):
        self.nc = nc
        self.q = {e: [] for e in self.ENGS}
        self.sems = {}
        self.waited = {e: {} for e in self.ENGS}

    def sem(self, ctx, name):
        h = ctx.enter_context(self.nc.semaphore(name))
        self.sems[name] = [h, 0]
        return name

    def cnt(self, name):
        return self.sems[name][1]

    def wait(self, eng, name, val):
        if val is None or val <= 0:
            return
        s = self.sems[name]
        assert val <= s[1], f"deadlock: wait {name}>={val} but only {s[1]} emitted"
        if self.waited[eng].get(name, 0) >= val:
            return
        self.waited[eng][name] = val
        self.q[eng].append(lambda e, h=s[0], v=val: e.wait_ge(h, v))

    def waits(self, eng, deps):
        for name, val in deps:
            self.wait(eng, name, val)

    def run(self, eng, fn, inc=None, n=1):
        if inc is not None:
            s = self.sems[inc]
            s[1] += n
            self.q[eng].append(lambda e, fn=fn, h=s[0], m=n: fn(e).then_inc(h, m))
            return s[1]
        self.q[eng].append(lambda e, fn=fn: fn(e))
        return None


def build_nc():
    nc = bass.Bass("TRN2", target_bir_lowering=False)

    xt = nc.dram_tensor("xt", [B * S, D], F32, kind="ExternalInput").ap()
    xf = nc.dram_tensor("xf", [B * D, S], F32, kind="ExternalInput").ap()
    w1 = nc.dram_tensor("w1", [D, D], F32, kind="ExternalInput").ap()
    w2 = nc.dram_tensor("w2", [D, D], F32, kind="ExternalInput").ap()
    w3 = nc.dram_tensor("w3", [D, D], F32, kind="ExternalInput").ap()
    w4 = nc.dram_tensor("w4", [D, D], F32, kind="ExternalInput").ap()
    wt = nc.dram_tensor("wt", [D, D], F32, kind="ExternalInput").ap()
    bqh = nc.dram_tensor("bqh", [128, 8], F32, kind="ExternalInput").ap()
    bkh = nc.dram_tensor("bkh", [128, 8], F32, kind="ExternalInput").ap()
    bvh = nc.dram_tensor("bvh", [128, 8], F32, kind="ExternalInput").ap()
    b4h = nc.dram_tensor("b4h", [128, D], F32, kind="ExternalInput").ap()
    idh = nc.dram_tensor("idh", [128, 128], BF16, kind="ExternalInput").ap()
    out = nc.dram_tensor("out", [B * S, D], F32, kind="ExternalOutput").ap()
    qts = nc.dram_tensor("qts", [B * 8 * 128, S], BF16, kind="Internal").ap()
    wbh = [nc.dram_tensor("wbh%d" % i, [D, D], BF16, kind="Internal").ap()
           for i in range(4)]
    xbh = nc.dram_tensor("xbh", [B * NTB * 128, 8 * TBW], BF16, kind="Internal").ap()

    ctx = ExitStack()
    with ctx:
        P = Prog(nc, ctx)
        for s_ in ("tt", "vv", "aa", "gg", "slc", "glw", "slw", "glx0", "glx1",
                   "slq0", "slq1", "slx", "slx2", "sxb", "gpc", "sq", "so"):
            P.sem(ctx, s_)

        def dma(eng, semname, fn, extra_waits=()):
            P.wait(eng, semname, P.cnt(semname))
            for nm, vl in extra_waits:
                P.wait(eng, nm, vl)
            return P.run(eng, fn, semname, 16)

        def sb(name, shape, dt=F32, c=ctx):
            return c.enter_context(nc.sbuf_tensor(name, shape, dt))

        KT = sb("KT", [128, 8, S], BF16)
        VP = sb("VP", [128, NST, D], BF16)
        wqT = sb("wqT", [128, 8, D], BF16)
        bq = sb("bq", [128, 8]); bk = sb("bk", [128, 8]); bv = sb("bv", [128, 8])
        b4b = sb("b4b", [128, D])
        ident = sb("ident", [128, 128], BF16)
        ssel = sb("ssel", [128, 1])
        ssm1 = sb("ssm1", [128, 1])
        epsln = sb("epsln", [128, 1])
        r127 = sb("r127", [128, 1])
        c1ap = sb("c1ap", [128, 1])
        nc1ap = sb("nc1ap", [128, 1])
        wscv = sb("wscv", [128, 1])
        stats = sb("stats", [128, 128])
        onesf = sb("onesf", [1, 128])

        for dst, src in ((bq, bqh), (bk, bkh), (bv, bvh), (b4b, b4h), (ident, idh)):
            dma("s", "slc", lambda e, d=dst, s_=src: e.dma_start(out=d[:], in_=s_))
        sl_static = P.cnt("slc")

        with ExitStack() as p0:
            wtf = sb("wtf", [128, 8, D], F32, p0)
            u0p = sb("u0p", [128, D], F32, p0)
            u1p = sb("u1p", [128, D], F32, p0)
            u2p = sb("u2p", [128, D], F32, p0)
            u3p = sb("u3p", [128, D], F32, p0)
            ps0 = p0.enter_context(nc.psum_tensor("ps0", [128, 1], F32))

            dma("s", "slc", lambda e: e.dma_start(
                out=wtf[:], in_=wt.rearrange("(c p) o -> p c o", p=128)))
            sl_wtf = P.cnt("slc")

            for wi, src in enumerate((w1, w2)):
                dma("g", "glw", lambda e, wi=wi, src=src: e.dma_start(
                    out=wbh[wi], in_=src))

            P.run("v", lambda e: e.memset(epsln[:], EPS_LN), "vv")
            P.run("v", lambda e: e.memset(c1ap[:], C1), "vv")
            P.run("v", lambda e: e.memset(nc1ap[:], -C1), "vv")
            v = P.run("v", lambda e: e.memset(stats[:, 122:123], 127.0), "vv")
            P.wait("v", "vv", v)
            v = P.run("v", lambda e: e.reciprocal(out=r127[:], in_=stats[:, 122:123]), "vv")
            P.wait("v", "vv", v)
            v = P.run("v", lambda e: e.memset(stats[:, 120:121], 1.25), "vv")
            P.wait("v", "vv", v)
            v = P.run("v", lambda e: e.tensor_scalar(
                out=stats[:, 121:122], in0=stats[:, 120:121], scalar1=C2,
                scalar2=None, op0=ADD), "vv")
            P.wait("v", "vv", v)
            v = P.run("v", lambda e: e.tensor_scalar(
                out=ssel[:], in0=stats[:, 121:122], scalar1=-C2,
                scalar2=None, op0=ADD), "vv")
            vv_ssel = v
            P.wait("v", "vv", v)
            v = P.run("v", lambda e: e.tensor_scalar(
                out=ssm1[:], in0=ssel[:], scalar1=-1.0, scalar2=1.0,
                op0=MUL, op1=ADD), "vv")

            P.wait("v", "slc", sl_wtf)
            for c in range(8):
                P.run("v", lambda e, c=c: e.tensor_reduce(
                    out=stats[:, 96 + c:97 + c], in_=wtf[:, c, :], axis=AX,
                    op=ADD, apply_absolute_value=True), "vv")
            v = P.cnt("vv")
            P.wait("v", "vv", v)
            v = P.run("v", lambda e: e.tensor_reduce(
                out=stats[:, 104:105], in_=stats[:, 96:104], axis=AX, op=ADD), "vv")
            P.wait("g", "vv", v)
            g = P.run("g", lambda e: e.tensor_reduce(
                out=stats[0:1, 105:106], in_=stats[:, 104:105], axis=AXC, op=ADD), "gg")
            for wi, src in enumerate((w3, w4)):
                dma("g", "glw", lambda e, wi=wi, src=src: e.dma_start(
                    out=wbh[wi + 2], in_=src))
            v = P.run("v", lambda e: e.memset(onesf[:], 1.0), "vv")
            P.wait("t", "vv", v)
            P.wait("t", "gg", g)
            t = P.run("t", lambda e: e.matmul(
                ps0[:], onesf[:], stats[0:1, 105:106], start=True, stop=True), "tt")
            P.wait("v", "tt", t)
            v = P.run("v", lambda e: e.tensor_copy(out=stats[:, 106:107], in_=ps0[:]), "vv")
            P.wait("v", "vv", v)
            v = P.run("v", lambda e: e.tensor_scalar(
                out=wscv[:], in0=stats[:, 106:107], scalar1=INVN, scalar2=EPS_Q,
                op0=MUL, op1=ADD), "vv")
            P.wait("v", "vv", v)
            v = P.run("v", lambda e: e.reciprocal(out=stats[:, 107:108], in_=wscv[:]), "vv")
            vv_rws = v

            pA = [u0p, u1p]
            pB = [u2p, u3p]
            P.wait("a", "slc", sl_wtf)
            P.wait("a", "vv", vv_rws)
            chunk_v_done = [0, 0]
            for c in range(8):
                s_ = c % 2
                P.wait("a", "vv", chunk_v_done[s_])
                a = P.run("a", lambda e, c=c, s_=s_: e.activation(
                    out=pA[s_][:], in_=wtf[:, c, :], func=AF.Identity,
                    scale=stats[:, 107:108], bias=0.0), "aa")
                P.wait("a", "aa", a)
                a = P.run("a", lambda e, s_=s_: e.activation(
                    out=pB[s_][:], in_=pA[s_][:], func=AF.Identity,
                    scale=1.0, bias=c1ap[:]), "aa")
                P.wait("a", "aa", a)
                a = P.run("a", lambda e, s_=s_: e.activation(
                    out=pA[s_][:], in_=pB[s_][:], func=AF.Identity,
                    scale=1.0, bias=nc1ap[:]), "aa")
                P.wait("v", "aa", a)
                v = P.run("v", lambda e, s_=s_: e.tensor_scalar(
                    out=pB[s_][:], in0=pA[s_][:], scalar1=C2, scalar2=None,
                    op0=ADD), "vv")
                P.wait("v", "vv", v)
                v = P.run("v", lambda e, s_=s_: e.scalar_tensor_tensor(
                    out=pB[s_][:], in0=pB[s_][:], scalar=-C2, op0=ADD,
                    in1=pA[s_][:], op1=SUB), "vv")
                P.wait("v", "vv", v)
                v = P.run("v", lambda e, s_=s_: e.scalar_tensor_tensor(
                    out=pB[s_][:], in0=pB[s_][:], scalar=ssel[:], op0=MUL,
                    in1=pA[s_][:], op1=ADD), "vv")
                P.wait("v", "vv", v)
                v = P.run("v", lambda e, c=c, s_=s_: e.tensor_scalar(
                    out=wqT[:, c, :], in0=pB[s_][:], scalar1=1.0, scalar2=-1.0,
                    op0=MINO, op1=MAXO), "vv")
                chunk_v_done[s_] = v
            vv_wq = P.cnt("vv")

        def barrier():
            marks = {n: P.cnt(n) for n in P.sems}
            for e in Prog.ENGS:
                for n, v_ in marks.items():
                    P.wait(e, n, v_)
        barrier()

        kt_ready = [0, 0]
        vp_ready = [0, 0]
        qt_stored = [[0] * NTB, [0] * NTB]
        xbh_ready = {}

        phase2_end_tt = 0

        for b in range(B):
            with ExitStack() as p1:
                W1b = sb("W1b%d" % b, [128, 8, D], BF16, p1)
                W2b = sb("W2b%d" % b, [128, 8, D], BF16, p1)
                xtb = [sb(f"xtb{b}_{i}", [128, 8, TBW], BF16, p1) for i in range(2)]
                QTs = sb("QTs%d" % b, [128, 8, TBW], BF16, p1)
                psQ = p1.enter_context(nc.psum_tensor("psQ%d" % b, [128, 4, TBW], F32))
                psK = p1.enter_context(nc.psum_tensor("psK%d" % b, [128, 4, TBW], F32))

                xtb_free = [0, 0]
                bank_free_q = [0] * 4
                bank_free_k = [0] * 4
                qts_free = 0
                gl_xt = [0, 0]
                sl_xt = [0, 0]
                sxb_buf = [0, 0]

                def load_xk(k):
                    if (b, k) in xbh_ready:
                        row = (b * NTB + k) * 128
                        gl_xt[k % 2] = dma(
                            "g", "glx%d" % (k % 2),
                            lambda e, row=row, k=k: e.dma_start(
                                out=xtb[k % 2][:, 0:4, :],
                                in_=xbh[row:row + 128, 0:4 * TBW]
                                .rearrange("p (c t) -> p c t", c=4)),
                            extra_waits=(("tt", xtb_free[k % 2]),
                                         xbh_ready[(b, k)]))
                        sl_xt[k % 2] = dma(
                            "s", "slx2",
                            lambda e, row=row, k=k: e.dma_start(
                                out=xtb[k % 2][:, 4:8, :],
                                in_=xbh[row:row + 128, 4 * TBW:8 * TBW]
                                .rearrange("p (c t) -> p c t", c=4)),
                            extra_waits=(("tt", xtb_free[k % 2]),
                                         xbh_ready[(b, k)]))
                    else:
                        gl_xt[k % 2] = dma(
                            "g", "glx%d" % (k % 2),
                            lambda e, b=b, k=k: e.dma_start(
                                out=xtb[k % 2][:],
                                in_=xf[b * D:(b + 1) * D, k * TBW:(k + 1) * TBW]
                                .rearrange("(c p) t -> p c t", p=128)),
                            extra_waits=(("tt", xtb_free[k % 2]),
                                         ("sxb", sxb_buf[k % 2])))
                        sl_xt[k % 2] = 0
                        row = (b * NTB + k) * 128
                        sxb_buf[k % 2] = dma(
                            "s", "sxb",
                            lambda e, row=row, k=k: e.dma_start(
                                out=xbh[row:row + 128, :]
                                .rearrange("p (c t) -> p c t", c=8),
                                in_=xtb[k % 2][:]),
                            extra_waits=(("glx%d" % (k % 2), gl_xt[k % 2]),))
                        xbh_ready[(b, k)] = ("sxb", sxb_buf[k % 2])

                dma("g", "glw", lambda e: e.dma_start(
                    out=W1b[:, 0:4, :],
                    in_=wbh[0][0:D // 2, :].rearrange("(c p) o -> p c o", p=128)))
                gl_w1 = P.cnt("glw")
                dma("s", "slw", lambda e: e.dma_start(
                    out=W1b[:, 4:8, :],
                    in_=wbh[0][D // 2:D, :].rearrange("(c p) o -> p c o", p=128)))
                sl_w1 = P.cnt("slw")
                load_xk(0)
                dma("g", "glw", lambda e: e.dma_start(
                    out=W2b[:, 0:4, :],
                    in_=wbh[1][0:D // 2, :].rearrange("(c p) o -> p c o", p=128)))
                gl_w2 = P.cnt("glw")
                dma("s", "slw", lambda e: e.dma_start(
                    out=W2b[:, 4:8, :],
                    in_=wbh[1][D // 2:D, :].rearrange("(c p) o -> p c o", p=128)))
                sl_w2 = P.cnt("slw")
                if NTB > 1:
                    load_xk(1)

                for k in range(NTB):
                    evq = []
                    for oc in range(8):
                        P.wait("t", "glw", gl_w1)
                        P.wait("t", "slw", sl_w1)
                        P.wait("t", "glx%d" % (k % 2), gl_xt[k % 2])
                        P.wait("t", "slx2", sl_xt[k % 2])
                        P.wait("t", "vv", bank_free_q[oc % 4])
                        for e_ in range(8):
                            tv = P.run("t", lambda e, oc=oc, e_=e_, k=k: e.matmul(
                                psQ[:, oc % 4, :],
                                W1b[:, e_, oc * 128:(oc + 1) * 128],
                                xtb[k % 2][:, e_, :],
                                start=(e_ == 0), stop=(e_ == 7)),
                                "tt" if e_ == 7 else None)
                        P.wait("v", "tt", tv)
                        P.wait("v", "sq", qts_free)
                        ev = P.run("v", lambda e, oc=oc: e.tensor_scalar(
                            out=QTs[:, oc, :], in0=psQ[:, oc % 4, :],
                            scalar1=bq[:, oc:oc + 1], scalar2=None, op0=ADD), "vv")
                        bank_free_q[oc % 4] = ev
                        evq.append(ev)
                    qt_stored[b][k] = dma(
                        "s", "sq",
                        lambda e, b=b, k=k: e.dma_start(
                            out=qts[b * 1024:(b + 1) * 1024, k * TBW:(k + 1) * TBW]
                            .rearrange("(c p) t -> p c t", p=128),
                            in_=QTs[:]),
                        extra_waits=(("vv", evq[-1]),))
                    qts_free = qt_stored[b][k]

                    for oc in range(8):
                        P.wait("t", "glw", gl_w2)
                        P.wait("t", "slw", sl_w2)
                        P.wait("t", "vv", bank_free_k[oc % 4])
                        for e_ in range(8):
                            tv = P.run("t", lambda e, oc=oc, e_=e_, k=k: e.matmul(
                                psK[:, oc % 4, :],
                                W2b[:, e_, oc * 128:(oc + 1) * 128],
                                xtb[k % 2][:, e_, :],
                                start=(e_ == 0), stop=(e_ == 7)),
                                "tt" if e_ == 7 else None)
                        P.wait("v", "tt", tv)
                        ev = P.run("v", lambda e, oc=oc, k=k: e.tensor_scalar(
                            out=KT[:, oc, k * TBW:(k + 1) * TBW], in0=psK[:, oc % 4, :],
                            scalar1=bk[:, oc:oc + 1], scalar2=None, op0=ADD), "vv")
                        bank_free_k[oc % 4] = ev
                    xtb_free[k % 2] = tv
                    if k + 2 < NTB:
                        load_xk(k + 2)
                kt_ready[b] = P.cnt("vv")
            barrier()

            with ExitStack() as p1v:
                W3b = sb("W3b%d" % b, [128, 8, D], BF16, p1v)
                W4b = sb("W4b%d" % b, [128, 8, D], BF16, p1v)
                xtb = [sb(f"xvb{b}_{i}", [128, 8, TBW], BF16, p1v) for i in range(2)]
                VTb = sb("VTb%d" % b, [128, 8, TBW], BF16, p1v)
                psV = p1v.enter_context(nc.psum_tensor("psV%d" % b, [128, 4, TBW], F32))
                psVp = p1v.enter_context(nc.psum_tensor("psVp%d" % b, [128, 2, TBW], F32))

                xtb_free = [0, 0]
                bank_free_v = [0] * 4
                bank_free_p = [0, 0]
                vtb_free = 0
                gl_xt = [0, 0]
                sl_xt = [0, 0]

                def load_xk(k):
                    row = (b * NTB + k) * 128
                    gl_xt[k % 2] = dma(
                        "g", "glx%d" % (k % 2),
                        lambda e, row=row, k=k: e.dma_start(
                            out=xtb[k % 2][:, 0:4, :],
                            in_=xbh[row:row + 128, 0:4 * TBW]
                            .rearrange("p (c t) -> p c t", c=4)),
                        extra_waits=(("tt", xtb_free[k % 2]),
                                     xbh_ready[(b, k)]))
                    sl_xt[k % 2] = dma(
                        "s", "slx2",
                        lambda e, row=row, k=k: e.dma_start(
                            out=xtb[k % 2][:, 4:8, :],
                            in_=xbh[row:row + 128, 4 * TBW:8 * TBW]
                            .rearrange("p (c t) -> p c t", c=4)),
                        extra_waits=(("tt", xtb_free[k % 2]),
                                     xbh_ready[(b, k)]))

                dma("g", "glw", lambda e: e.dma_start(
                    out=W3b[:, 0:4, :],
                    in_=wbh[2][0:D // 2, :].rearrange("(c p) o -> p c o", p=128)))
                gl_w3 = P.cnt("glw")
                dma("s", "slw", lambda e: e.dma_start(
                    out=W3b[:, 4:8, :],
                    in_=wbh[2][D // 2:D, :].rearrange("(c p) o -> p c o", p=128)))
                sl_w3 = P.cnt("slw")
                load_xk(0)
                dma("g", "glw", lambda e: e.dma_start(
                    out=W4b[:, 0:4, :],
                    in_=wbh[3][0:D // 2, :].rearrange("(c p) o -> p c o", p=128)))
                gl_w4 = P.cnt("glw")
                dma("s", "slw", lambda e: e.dma_start(
                    out=W4b[:, 4:8, :],
                    in_=wbh[3][D // 2:D, :].rearrange("(c p) o -> p c o", p=128)))
                sl_w4 = P.cnt("slw")
                if NTB > 1:
                    load_xk(1)

                for k in range(NTB):
                    vt_evacs = []
                    for oc in range(8):
                        P.wait("t", "glw", gl_w3)
                        P.wait("t", "slw", sl_w3)
                        P.wait("t", "glx%d" % (k % 2), gl_xt[k % 2])
                        P.wait("t", "slx2", sl_xt[k % 2])
                        P.wait("t", "vv", bank_free_v[oc % 4])
                        for e_ in range(8):
                            tv = P.run("t", lambda e, oc=oc, e_=e_, k=k: e.matmul(
                                psV[:, oc % 4, :],
                                W3b[:, e_, oc * 128:(oc + 1) * 128],
                                xtb[k % 2][:, e_, :],
                                start=(e_ == 0), stop=(e_ == 7)),
                                "tt" if e_ == 7 else None)
                        P.wait("v", "tt", tv)
                        P.wait("v", "tt", vtb_free)
                        ev = P.run("v", lambda e, oc=oc: e.tensor_scalar(
                            out=VTb[:, oc, :], in0=psV[:, oc % 4, :],
                            scalar1=bv[:, oc:oc + 1], scalar2=None, op0=ADD), "vv")
                        bank_free_v[oc % 4] = ev
                        vt_evacs.append(ev)
                    xtb_free[k % 2] = tv

                    for tt_ in range(4):
                        for h in range(2):
                            P.wait("t", "glw", gl_w4)
                            P.wait("t", "slw", sl_w4)
                            P.wait("t", "vv", vt_evacs[-1])
                            P.wait("t", "vv", bank_free_p[h])
                            for e_ in range(8):
                                tv = P.run("t", lambda e, tt_=tt_, h=h, e_=e_: e.matmul(
                                    psVp[:, h, :],
                                    VTb[:, e_, tt_ * 128:(tt_ + 1) * 128],
                                    W4b[:, e_, h * TBW:(h + 1) * TBW],
                                    start=(e_ == 0), stop=(e_ == 7)),
                                    "tt" if e_ == 7 else None)
                            P.wait("v", "tt", tv)
                            ev = P.run("v", lambda e, k=k, tt_=tt_, h=h: e.tensor_tensor(
                                out=VP[:, k * 4 + tt_, h * TBW:(h + 1) * TBW],
                                in0=psVp[:, h, :],
                                in1=b4b[:, h * TBW:(h + 1) * TBW], op=ADD), "vv")
                            bank_free_p[h] = ev
                    vtb_free = tv
                    if k + 2 < NTB:
                        load_xk(k + 2)
                vp_ready[b] = P.cnt("vv")
            barrier()

            with ExitStack() as p2:
                scor = sb("scor%d" % b, [128, S], F32, p2)
                Pb = sb("Pb%d" % b, [128, S], BF16, p2)
                PT = sb("PT%d" % b, [128, NST, 128], BF16, p2)
                qtt = sb("qtt%d" % b, [128, 8, 128], BF16, p2)
                attn = sb("attn%d" % b, [128, D], F32, p2)
                x1bufs = [sb(f"x1_{b}_{i}", [128, D], F32, p2) for i in range(2)]
                r2b = sb("r2b%d" % b, [128, D], F32, p2)
                if S >= 3072 + 1024:
                    outbuf = scor[:, 0:1024]
                    u2 = scor[:, 2048:3072]
                    u3 = scor[:, 3072:4096]
                else:
                    outbuf = sb("outb%d" % b, [128, D], F32, p2)
                    u2 = sb("u2b%d" % b, [128, D], F32, p2)
                    u3 = sb("u3b%d" % b, [128, D], F32, p2)
                aqbufs = [sb(f"aqb{b}_{i}", [128, D], BF16, p2) for i in range(2)]
                aqT = sb("aqT%d" % b, [128, 8, 128], BF16, p2)
                psS = p2.enter_context(nc.psum_tensor("psS%d" % b, [128, 4, TBW], F32))
                psT0 = p2.enter_context(nc.psum_tensor("psT0%d" % b, [128, GSZ, 128], BF16))
                psT1 = p2.enter_context(nc.psum_tensor("psT1%d" % b, [128, GSZ, 128], BF16))
                psO = p2.enter_context(nc.psum_tensor("psO%d" % b, [128, 2, TBW], F32))
                psTs = [psT0, psT1]

                sc_stop = {}
                sc_evac_all = {}
                sc_last = {}
                pt_evac = {}
                pt_stop = {}
                pt_last = {-1: 0, -2: 0}
                pv_stop = {}
                pv_q = {}
                attn_evac = {}
                exp_done = {-1: 0}
                rrs_done = {}
                x1_done = {-1: 0}
                aq_done = {-1: 0}
                radd_done = {}
                rfuse = {}
                zc_a_done = {}
                r2h_done = {-1: 0, -2: 0}
                bn2_done = {-1: 0, -2: 0}
                out_done = {-1: 0, -2: 0}
                so_done = {-1: 0, -2: 0}
                ffn_stop = {}
                aqtr_done = {-1: 0}
                aqT_evac = {-1: 0}
                qt_ld = {}
                x_ld = {}
                ysc_done = {-1: 0, -2: 0}
                bn1_done = {}
                psT_free = {0: 0, 1: 0}
                stats_free = {0: (0, 0), 1: (0, 0)}

                def qt_load(i):
                    qt_ld[i] = dma(
                        "g", "slq0",
                        lambda e, b=b, i=i: e.dma_start(
                            out=qtt[:],
                            in_=qts[b * 1024:(b + 1) * 1024, i * 128:(i + 1) * 128]
                            .rearrange("(c p) t -> p c t", p=128)),
                        extra_waits=(("sq", qt_stored[b][NTB - 1]),
                                     ("tt", sc_last.get(i - 1, 0))))

                qt_load(0)

                for it in range(NST + 3):
                    i = it
                    ie = it - 1
                    if_ = it - 3
                    p = i % 2
                    pe = ie % 2
                    sbase = 64 * p
                    sbe = 64 * pe
                    sbf = 64 * (if_ % 2)
                    x1e = x1bufs[ie % 2]
                    x1f = x1bufs[if_ % 2]
                    aqe = aqbufs[ie % 2]
                    aqf = aqbufs[if_ % 2]

                    def sc_block(j):
                        P.wait("t", "slq0", qt_ld[i])
                        P.wait("t", "vv", kt_ready[b])
                        mb0 = sbase + 0
                        pg = i * NSB + j - 4
                        if pg >= 0:
                            P.wait("t", "vv",
                                   sc_evac_all.get((pg // NSB, pg % NSB), 0))
                        for e_ in range(8):
                            tv = P.run("t", lambda e, i=i, j=j, e_=e_: e.matmul(
                                psS[:, j % 4, :],
                                qtt[:, e_, :],
                                KT[:, e_, j * TBW:(j + 1) * TBW],
                                start=(e_ == 0), stop=(e_ == 7)),
                                "tt" if e_ == 7 else None)
                        sc_stop[(i, j)] = tv
                        if j == 0:
                            P.wait("v", "aa", exp_done.get(i - 1, 0))
                            P.wait("v", "vv", out_done.get(i - 4, 0))
                            P.wait("v", "so", so_done.get(i - 4, 0))
                            P.wait("v", "vv", aq_done.get(i - 2, 0))
                            P.wait("v", "vv", rfuse.get((i - 2, 1), 0))
                        if j % 2 == 1 or NSB == 1:
                            P.wait("v", "tt", tv)
                            jp = j // 2
                            j0_, wid = (j - 1, 2) if NSB > 1 else (j, 1)
                            ev = P.run("v", lambda e, j0_=j0_, wid=wid, jp=jp, mb0=mb0: e.tensor_scalar(
                                out=scor[:, j0_ * TBW:(j0_ + wid) * TBW],
                                in0=psS[:, j0_ % 4:j0_ % 4 + wid, :],
                                scalar1=1.0, scalar2=NEG_BIG, op0=MUL, op1=MAXO,
                                accum_out=stats[:, mb0 + jp:mb0 + jp + 1]), "vv")
                            sc_evac_all[(i, j0_)] = ev
                            sc_evac_all[(i, j)] = ev

                    if i < NST:
                        for j in range(NSB // 2):
                            sc_block(j)

                    iq = it - 2
                    if 0 <= iq < NST:
                        sbq = 64 * (iq % 2)
                        x1q = x1bufs[iq % 2]
                        aqq = aqbufs[iq % 2]
                        P.wait("v", "vv", bn1_done[iq])
                        b6 = sbq + 12
                        v = P.run("v", lambda e, b6=b6, sbq=sbq: e.bn_aggr(
                            out=stats[:, sbq + 24:sbq + 26], in_=stats[:, b6:b6 + 12]), "vv")
                        P.wait("a", "vv", v)
                        a = P.run("a", lambda e, sbq=sbq: e.activation(
                            out=stats[:, sbq + 27:sbq + 28], in_=stats[:, sbq + 25:sbq + 26],
                            func=AF.Ln, bias=epsln[:], scale=1.0), "aa")
                        P.wait("a", "aa", a)
                        a = P.run("a", lambda e, sbq=sbq: e.activation(
                            out=stats[:, sbq + 28:sbq + 29], in_=stats[:, sbq + 27:sbq + 28],
                            func=AF.Exp, bias=0.0, scale=-0.5), "aa")
                        rstd_a = a
                        P.wait("v", "aa", rstd_a)
                        P.wait("v", "vv", ysc_done.get(iq - 2, 0))
                        P.run("v", lambda e, sbq=sbq: e.tensor_reduce(
                            out=stats[:, sbq + 58:sbq + 59], in_=stats[:, sbq + 54:sbq + 56],
                            axis=AX, op=MAXO), "vv")
                        v = P.run("v", lambda e, sbq=sbq: e.tensor_reduce(
                            out=stats[:, sbq + 59:sbq + 60], in_=stats[:, sbq + 56:sbq + 58],
                            axis=AX, op=MINO), "vv")
                        P.wait("v", "vv", v)
                        v = P.run("v", lambda e, sbq=sbq: e.tensor_scalar(
                            out=stats[:, sbq + 60:sbq + 62], in0=stats[:, sbq + 58:sbq + 60],
                            scalar1=stats[:, sbq + 24:sbq + 25], scalar2=None,
                            op0=SUB), "vv")
                        P.wait("v", "vv", v)
                        v = P.run("v", lambda e, sbq=sbq: e.tensor_reduce(
                            out=stats[:, sbq + 62:sbq + 63], in_=stats[:, sbq + 60:sbq + 62],
                            axis=AX, op=MAXO, apply_absolute_value=True), "vv")
                        P.wait("v", "vv", v)
                        v = P.run("v", lambda e, sbq=sbq: e.tensor_scalar(
                            out=stats[:, sbq + 31:sbq + 32], in0=stats[:, sbq + 62:sbq + 63],
                            scalar1=stats[:, sbq + 28:sbq + 29], scalar2=EPS_Q,
                            op0=MUL, op1=ADD), "vv")
                        P.wait("v", "vv", v)
                        v = P.run("v", lambda e, sbq=sbq: e.reciprocal(
                            out=stats[:, sbq + 32:sbq + 33], in_=stats[:, sbq + 31:sbq + 32]), "vv")
                        P.wait("v", "vv", v)
                        v = P.run("v", lambda e, sbq=sbq: e.tensor_scalar(
                            out=stats[:, sbq + 33:sbq + 34], in0=stats[:, sbq + 32:sbq + 33],
                            scalar1=127.0, scalar2=stats[:, sbq + 28:sbq + 29],
                            op0=MUL, op1=MUL), "vv")
                        zq_v = v
                        P.wait("v", "vv", zq_v)
                        v = P.run("v", lambda e, sbq=sbq: e.scalar_tensor_tensor(
                            out=stats[:, sbq + 63:sbq + 64], in0=stats[:, sbq + 24:sbq + 25],
                            scalar=-1.0, op0=MUL, in1=stats[:, sbq + 33:sbq + 34],
                            op1=MUL), "vv")
                        nmq_v = v
                        v = P.run("v", lambda e, sbq=sbq: e.scalar_tensor_tensor(
                            out=stats[:, sbq + 29:sbq + 30], in0=stats[:, sbq + 24:sbq + 25],
                            scalar=-1.0, op0=MUL, in1=stats[:, sbq + 28:sbq + 29],
                            op1=MUL), "vv")
                        nmr_v = v
                        P.wait("a", "vv", nmr_v)
                        P.wait("a", "vv", out_done.get(iq - 2, 0))
                        a = P.run("a", lambda e, sbq=sbq, x1q=x1q: e.activation(
                            out=x1q[:], in_=attn[:], func=AF.Identity,
                            scale=stats[:, sbq + 28:sbq + 29],
                            bias=stats[:, sbq + 29:sbq + 30]), "aa")
                        x1_done[iq] = a

                    if if_ >= 0:
                        for h in range(2):
                            P.wait("t", "aa", aqT_evac[if_])
                            P.wait("t", "vv", vv_wq)
                            P.wait("t", "vv", r2h_done.get(if_ - 1, 0))
                            P.wait("t", "vv", rfuse.get((ie - 1, h), 0))
                            for ic in range(8):
                                tv = P.run("t", lambda e, ic=ic, h=h: e.matmul(
                                    psO[:, h, :], aqT[:, ic, :],
                                    wqT[:, ic, h * TBW:(h + 1) * TBW],
                                    start=(ic == 0), stop=(ic == 7)),
                                    "tt" if ic == 7 else None)
                            ffn_stop[(if_, h)] = tv
                        v = P.run("v", lambda e, sbf=sbf: e.tensor_scalar(
                            out=stats[:, sbf + 34:sbf + 35], in0=stats[:, sbf + 31:sbf + 32],
                            scalar1=wscv[:], scalar2=r127[:], op0=MUL, op1=MUL), "vv")
                        ysc_done[if_] = v
                        P.wait("v", "tt", ffn_stop[(if_, 1)])
                        P.wait("v", "vv", v)
                        v2 = P.run("v", lambda e, sbf=sbf, x1f=x1f: e.scalar_tensor_tensor(
                            out=x1f[:], in0=psO[:, 0:2, :],
                            scalar=stats[:, sbf + 34:sbf + 35], op0=MUL,
                            in1=x1f[:], op1=ADD), "vv")
                        r2h_done[if_] = v2

                    if i < NST:
                        for j in range(NSB // 2, NSB):
                            sc_block(j)

                    if i < NST:
                        sc_last[i] = sc_stop[(i, NSB - 1)]
                        P.wait("v", "vv", sc_evac_all[(i, NSB - 1)])
                        v = P.run("v", lambda e, sbase=sbase: e.tensor_reduce(
                            out=stats[:, sbase + 8:sbase + 9],
                            in_=stats[:, sbase:sbase + NSB],
                            axis=AX, op=MAXO), "vv")
                        P.wait("v", "vv", v)
                        v = P.run("v", lambda e, sbase=sbase: e.tensor_scalar(
                            out=stats[:, sbase + 9:sbase + 10],
                            in0=stats[:, sbase + 8:sbase + 9],
                            scalar1=-INV32, scalar2=None, op0=MUL), "vv")
                        negm_v = v
                        P.wait("a", "vv", negm_v)
                        P.wait("a", "tt", pt_last.get(i - 1, 0))
                        a = P.run("a", lambda e, sbase=sbase: e.activation(
                            out=Pb[:], in_=scor[:], func=AF.Exp, scale=INV32,
                            bias=stats[:, sbase + 9:sbase + 10],
                            accum_out=stats[:, sbase + 10:sbase + 11]), "aa")
                        exp_done[i] = a
                        P.wait("v", "aa", a)
                        v = P.run("v", lambda e, sbase=sbase: e.reciprocal(
                            out=stats[:, sbase + 11:sbase + 12],
                            in_=stats[:, sbase + 10:sbase + 11]), "vv")
                        rrs_done[i] = v
                        if i + 1 < NST:
                            qt_load(i + 1)

                    if 0 <= iq < NST:
                        P.wait("v", "vv", nmq_v)
                        P.wait("v", "aa", x1_done[iq])
                        v = P.run("v", lambda e, sbq=sbq, u2=u2: e.tensor_scalar(
                            out=u2[:], in0=attn[:], scalar1=stats[:, sbq + 33:sbq + 34],
                            scalar2=stats[:, sbq + 63:sbq + 64], op0=MUL, op1=ADD), "vv")
                        P.wait("v", "vv", v)
                        v = P.run("v", lambda e, u2=u2: e.tensor_scalar(
                            out=attn[:], in0=u2[:], scalar1=127.0, scalar2=-128.0,
                            op0=MINO, op1=MAXO), "vv")
                        P.wait("v", "vv", v)
                        v = P.run("v", lambda e, u2=u2: e.tensor_scalar(
                            out=u2[:], in0=attn[:], scalar1=C1, scalar2=None,
                            op0=ADD), "vv")
                        P.wait("v", "vv", v)
                        v = P.run("v", lambda e, u2=u2: e.tensor_scalar(
                            out=attn[:], in0=u2[:], scalar1=-C1, scalar2=None,
                            op0=ADD), "vv")
                        P.wait("v", "vv", v)
                        v = P.run("v", lambda e, u2=u2: e.tensor_scalar(
                            out=u2[:], in0=attn[:], scalar1=C2, scalar2=-C2,
                            op0=ADD, op1=ADD), "vv")
                        P.wait("v", "vv", v)
                        v = P.run("v", lambda e, u2=u2, u3=u3: e.tensor_scalar(
                            out=u3[:], in0=u2[:], scalar1=ssel[:], scalar2=None,
                            op0=MUL), "vv")
                        P.wait("v", "vv", v)
                        P.wait("v", "tt", aqtr_done.get(iq - 2, 0))
                        v = P.run("v", lambda e, aqq=aqq, u3=u3: e.scalar_tensor_tensor(
                            out=aqq[:], in0=attn[:], scalar=ssm1[:], op0=MUL,
                            in1=u3[:], op1=ADD), "vv")
                        aq_done[iq] = v

                    def pv_group(h, q):
                        if h == 0 and q == 0:
                            P.wait("t", "vv", vp_ready[b])
                            P.wait("t", "vv", rfuse.get((ie - 1, 0), 0))
                            P.wait("t", "vv", rfuse.get((ie - 1, 1), 0))
                            P.wait("t", "vv", r2h_done.get(if_, 0))
                        P.wait("t", "aa", pt_evac.get((ie, q), 0))
                        tv = None
                        for tc in range(q * GSZ, (q + 1) * GSZ):
                            tv = P.run("t", lambda e, tc=tc, h=h: e.matmul(
                                psO[:, h, :], PT[:, tc, :],
                                VP[:, tc, h * TBW:(h + 1) * TBW],
                                start=(tc == 0), stop=(tc == NST - 1),
                                skip_group_check=True),
                                "tt" if (tc == NST - 1 or tc == (q + 1) * GSZ - 1) else None)
                        if q == NG - 1:
                            pv_stop[(ie, h)] = tv
                        pv_q[(ie, h, q)] = tv

                    def attn_evac_h(h):
                        P.wait("v", "tt", pv_stop[(ie, h)])
                        P.wait("v", "vv", rrs_done[ie])
                        P.wait("v", "vv", aq_done.get(ie - 1, 0))
                        P.wait("v", "slx", x_ld[ie])
                        rf = P.run("v", lambda e, h=h, sbe=sbe: e.scalar_tensor_tensor(
                            out=attn[:, h * TBW:(h + 1) * TBW], in0=psO[:, h, :],
                            scalar=stats[:, sbe + 11:sbe + 12], op0=MUL,
                            in1=r2b[:, h * TBW:(h + 1) * TBW], op1=ADD), "vv")
                        rfuse[(ie, h)] = rf
                        P.wait("v", "vv", rf)
                        b6 = sbe + 12
                        P.run("v", lambda e, h=h, sbe=sbe: e.tensor_reduce(
                            out=stats[:, sbe + 54 + h:sbe + 55 + h],
                            in_=attn[:, h * TBW:(h + 1) * TBW], axis=AX, op=MAXO), "vv")
                        P.run("v", lambda e, h=h, sbe=sbe: e.tensor_reduce(
                            out=stats[:, sbe + 56 + h:sbe + 57 + h],
                            in_=attn[:, h * TBW:(h + 1) * TBW], axis=AX, op=MINO), "vv")
                        return P.run("v", lambda e, h=h, b6=b6: e.bn_stats(
                            out=stats[:, b6 + 6 * h:b6 + 6 * h + 6],
                            in_=attn[:, h * TBW:(h + 1) * TBW]), "vv")

                    def pt_group(g_):
                        P.wait("t", "aa", exp_done[i])
                        P.wait("t", "aa", psT_free[g_ % 2])
                        for kk in range(GSZ):
                            tv = P.run("t", lambda e, g_=g_, kk=kk: e.transpose(
                                psTs[g_ % 2][:, kk, :],
                                Pb[:, (g_ * GSZ + kk) * 128:(g_ * GSZ + kk + 1) * 128],
                                ident[:]), "tt" if kk == GSZ - 1 else None)
                        pt_stop[(i, g_)] = tv
                        P.wait("a", "tt", tv)
                        P.wait("a", "tt", pv_q.get((ie, 1, g_), 0))
                        ev = P.run("a", lambda e, g_=g_: e.activation(
                            out=PT[:, g_ * GSZ:(g_ + 1) * GSZ, :], in_=psTs[g_ % 2][:],
                            func=AF.Identity, bias=0.0, scale=1.0), "aa")
                        pt_evac[(i, g_)] = ev
                        psT_free[g_ % 2] = ev

                    if 0 <= ie and i < NST and NG >= 4:
                        for q in range(NG):
                            pv_group(0, q)
                        attn_evac_h(0)
                        pv_group(1, 0)
                        pt_group(0)
                        pv_group(1, 1)
                        pt_group(1)
                        pv_group(1, 2)
                        pt_group(2)
                        pv_group(1, 3)
                        pt_group(3)
                        bn1_done[ie] = attn_evac_h(1)
                    else:
                        if 0 <= ie < NST:
                            for h in range(2):
                                for q in range(NG):
                                    pv_group(h, q)
                            attn_evac_h(0)
                            bn1_done[ie] = attn_evac_h(1)
                        if i < NST:
                            for g_ in range(NG):
                                pt_group(g_)
                    if i < NST:
                        pt_last[i] = pt_stop[(i, NG - 1)]

                    ic_ = it - 2
                    if 0 <= ic_ < NST:
                        aqc = aqbufs[ic_ % 2]
                        P.wait("t", "vv", aq_done[ic_])
                        for ga in range((8 + GSZ - 1) // GSZ):
                            lo, hi = ga * GSZ, min((ga + 1) * GSZ, 8)
                            P.wait("t", "aa", psT_free[0])
                            for kk in range(lo, hi):
                                tv = P.run("t", lambda e, kk=kk, lo=lo, aqc=aqc: e.transpose(
                                    psT0[:, kk - lo, :], aqc[:, kk * 128:(kk + 1) * 128],
                                    ident[:]), "tt" if kk == hi - 1 else None)
                            P.wait("a", "tt", tv)
                            P.wait("a", "tt", ffn_stop.get((ic_ - 1, 1), 0))
                            ev = P.run("a", lambda e, lo=lo, hi=hi: e.activation(
                                out=aqT[:, lo:hi, :], in_=psT0[:, 0:hi - lo, :],
                                func=AF.Identity, bias=0.0, scale=1.0), "aa")
                            psT_free[0] = ev
                        aqtr_done[ic_] = tv
                        aqT_evac[ic_] = ev

                    if if_ >= 0:
                        P.wait("v", "vv", r2h_done[if_])
                        b62 = sbf + 36
                        P.run("v", lambda e, b62=b62, x1f=x1f: e.bn_stats(
                            out=stats[:, b62:b62 + 6], in_=x1f[:, 0:512]), "vv")
                        v = P.run("v", lambda e, b62=b62, x1f=x1f: e.bn_stats(
                            out=stats[:, b62 + 6:b62 + 12], in_=x1f[:, 512:1024]), "vv")
                        P.wait("v", "vv", v)
                        v = P.run("v", lambda e, b62=b62, sbf=sbf: e.bn_aggr(
                            out=stats[:, sbf + 48:sbf + 50], in_=stats[:, b62:b62 + 12]), "vv")
                        bn2_done[if_] = v
                        P.wait("a", "vv", v)
                        a = P.run("a", lambda e, sbf=sbf: e.activation(
                            out=stats[:, sbf + 51:sbf + 52], in_=stats[:, sbf + 49:sbf + 50],
                            func=AF.Ln, bias=epsln[:], scale=1.0), "aa")
                        P.wait("a", "aa", a)
                        a = P.run("a", lambda e, sbf=sbf: e.activation(
                            out=stats[:, sbf + 52:sbf + 53], in_=stats[:, sbf + 51:sbf + 52],
                            func=AF.Exp, bias=0.0, scale=-0.5), "aa")
                        P.wait("v", "aa", a)
                        v = P.run("v", lambda e, sbf=sbf: e.scalar_tensor_tensor(
                            out=stats[:, sbf + 53:sbf + 54], in0=stats[:, sbf + 48:sbf + 49],
                            scalar=-1.0, op0=MUL, in1=stats[:, sbf + 52:sbf + 53],
                            op1=MUL), "vv")
                        P.wait("v", "vv", v)
                        P.wait("v", "so", so_done.get(if_ - 1, 0))
                        v = P.run("v", lambda e, sbf=sbf, x1f=x1f: e.tensor_scalar(
                            out=outbuf[:], in0=x1f[:],
                            scalar1=stats[:, sbf + 52:sbf + 53],
                            scalar2=stats[:, sbf + 53:sbf + 54],
                            op0=MUL, op1=ADD), "vv")
                        out_done[if_] = v
                        so_done[if_] = dma(
                            "s", "so",
                            lambda e, b=b, if_=if_: e.dma_start(
                                out=out[b * S + if_ * 128:b * S + (if_ + 1) * 128, :],
                                in_=outbuf[:]),
                            extra_waits=(("vv", v),))

                    if i < NST:
                        x_ld[i] = dma(
                            "g", "slx",
                            lambda e, b=b, i=i: e.dma_start(
                                out=r2b[:],
                                in_=xt[b * S + i * 128:b * S + (i + 1) * 128, :]),
                            extra_waits=(("vv", rfuse.get((i - 1, 1), 0)),))

                    nq = 4 * NTB
                    if b + 1 < B and it < nq:
                        kx, qx = it // 4, it % 4
                        row = ((b + 1) * NTB + kx) * 128
                        g = dma("g", "gpc",
                                lambda e, b=b, kx=kx, qx=qx, row=row: e.dma_start(
                                    out=xbh[row:row + 128,
                                            qx * 2 * TBW:(qx + 1) * 2 * TBW]
                                    .rearrange("p (c t) -> p c t", c=2),
                                    in_=xf[(b + 1) * D + qx * 256:
                                           (b + 1) * D + (qx + 1) * 256,
                                           kx * TBW:(kx + 1) * TBW]
                                    .rearrange("(c p) t -> p c t", p=128)))
                        if qx == 3:
                            xbh_ready[(b + 1, kx)] = ("gpc", g)

                phase2_end_tt = P.cnt("tt")
            barrier()

        P.wait("s", "so", P.cnt("so"))

        with nc.Block() as block:
            @block.tensor
            def _(e):
                for f in P.q["t"]:
                    f(e)

            @block.vector
            def _(e):
                for f in P.q["v"]:
                    f(e)

            @block.scalar
            def _(e):
                for f in P.q["a"]:
                    f(e)

            @block.gpsimd
            def _(e):
                for f in P.q["g"]:
                    f(e)

            @block.sync
            def _(e):
                for f in P.q["s"]:
                    f(e)

    return nc


_NC_CACHE = None


def kernel(L0_attn_x, L0_attn_W1, L0_attn_b1, L0_attn_W2, L0_attn_b2,
           L0_attn_W3, L0_attn_b3, L0_attn_W4, L0_attn_b4, L0_ffn_weight):
    global _NC_CACHE
    x = np.ascontiguousarray(np.asarray(L0_attn_x, dtype=np.float32))
    W = [np.ascontiguousarray(np.asarray(w, dtype=np.float32))
         for w in (L0_attn_W1, L0_attn_W2, L0_attn_W3, L0_attn_W4)]
    bs = [np.ascontiguousarray(np.asarray(v, dtype=np.float32))
          for v in (L0_attn_b1, L0_attn_b2, L0_attn_b3, L0_attn_b4)]
    wf = np.ascontiguousarray(np.asarray(L0_ffn_weight, dtype=np.float32))
    wt = np.ascontiguousarray(wf.T)

    bqh = np.ascontiguousarray(bs[0].reshape(8, 128).T)
    bkh = np.ascontiguousarray(bs[1].reshape(8, 128).T)
    bvh = np.ascontiguousarray(bs[2].reshape(8, 128).T)
    b4h = np.ascontiguousarray(np.broadcast_to(bs[3], (128, D)))
    idh = np.eye(128, dtype=ml_dtypes.bfloat16)

    if _NC_CACHE is None:
        _NC_CACHE = build_nc()
    nc = _NC_CACHE

    in_maps = []
    for c in range(8):
        xs = x[2 * c:2 * c + 2]
        in_maps.append({
            "xt": np.ascontiguousarray(xs.reshape(B * S, D)),
            "xf": np.ascontiguousarray(xs.transpose(0, 2, 1).reshape(B * D, S)),
            "w1": W[0], "w2": W[1], "w3": W[2], "w4": W[3],
            "wt": wt, "bqh": bqh, "bkh": bkh, "bvh": bvh, "b4h": b4h,
            "idh": idh,
        })
    res = run_bass_kernel_spmd(nc, in_maps, core_ids=list(range(8)))
    outs = [np.asarray(res.results[c]["out"], dtype=np.float32).reshape(B, S, D)
            for c in range(8)]
    return np.concatenate(outs, axis=0)


# version 30 (final)
# speedup vs baseline: 1.1262x; 1.0051x over previous
import numpy as np
import ml_dtypes
from contextlib import ExitStack

import concourse.bass as bass
import concourse.mybir as mybir
from concourse.bass_utils import run_bass_kernel_spmd
from concourse.alu_op_type import AluOpType

F32 = mybir.dt.float32
BF16 = mybir.dt.bfloat16
AX = mybir.AxisListType.X
AXC = mybir.AxisListType.C
AF = mybir.ActivationFunctionType
ADD, SUB, MUL = AluOpType.add, AluOpType.subtract, AluOpType.mult
MAXO, MINO = AluOpType.max, AluOpType.min

C1 = float(3 << 22)
C2 = float(3 << 51)
NEG_BIG = -3.0e38
INV32 = 0.03125
INV127 = float(np.float32(np.float64(1.0) / np.float64(127.0)))
INVN = float(np.float32(1.0) / np.float32(1048576.0))
EPS_LN = 1e-5
EPS_Q = 1e-8

B, S, D = 2, 4096, 1024
NTB, TBW = 8, 512
NST = 32
NSB = 8
GSZ = 8
NG = 4


class Prog:
    ENGS = ("t", "v", "a", "g", "s")

    def __init__(self, nc, ctx):
        self.nc = nc
        self.q = {e: [] for e in self.ENGS}
        self.sems = {}
        self.waited = {e: {} for e in self.ENGS}

    def sem(self, ctx, name):
        h = ctx.enter_context(self.nc.semaphore(name))
        self.sems[name] = [h, 0]
        return name

    def cnt(self, name):
        return self.sems[name][1]

    def wait(self, eng, name, val):
        if val is None or val <= 0:
            return
        s = self.sems[name]
        assert val <= s[1], f"deadlock: wait {name}>={val} but only {s[1]} emitted"
        if self.waited[eng].get(name, 0) >= val:
            return
        self.waited[eng][name] = val
        self.q[eng].append(lambda e, h=s[0], v=val: e.wait_ge(h, v))

    def waits(self, eng, deps):
        for name, val in deps:
            self.wait(eng, name, val)

    def run(self, eng, fn, inc=None, n=1):
        if inc is not None:
            s = self.sems[inc]
            s[1] += n
            self.q[eng].append(lambda e, fn=fn, h=s[0], m=n: fn(e).then_inc(h, m))
            return s[1]
        self.q[eng].append(lambda e, fn=fn: fn(e))
        return None


def build_nc():
    nc = bass.Bass("TRN2", target_bir_lowering=False)

    xt = nc.dram_tensor("xt", [B * S, D], F32, kind="ExternalInput").ap()
    xf = nc.dram_tensor("xf", [B * D, S], F32, kind="ExternalInput").ap()
    w1 = nc.dram_tensor("w1", [D, D], F32, kind="ExternalInput").ap()
    w2 = nc.dram_tensor("w2", [D, D], F32, kind="ExternalInput").ap()
    w3 = nc.dram_tensor("w3", [D, D], F32, kind="ExternalInput").ap()
    w4 = nc.dram_tensor("w4", [D, D], F32, kind="ExternalInput").ap()
    wt = nc.dram_tensor("wt", [D, D], F32, kind="ExternalInput").ap()
    bqh = nc.dram_tensor("bqh", [128, 8], F32, kind="ExternalInput").ap()
    bkh = nc.dram_tensor("bkh", [128, 8], F32, kind="ExternalInput").ap()
    bvh = nc.dram_tensor("bvh", [128, 8], F32, kind="ExternalInput").ap()
    b4h = nc.dram_tensor("b4h", [128, D], F32, kind="ExternalInput").ap()
    idh = nc.dram_tensor("idh", [128, 128], BF16, kind="ExternalInput").ap()
    out = nc.dram_tensor("out", [B * S, D], F32, kind="ExternalOutput").ap()
    qts = nc.dram_tensor("qts", [B * 8 * 128, S], BF16, kind="Internal").ap()
    wbh = [nc.dram_tensor("wbh%d" % i, [D, D], BF16, kind="Internal").ap()
           for i in range(4)]
    xbh = nc.dram_tensor("xbh", [B * NTB * 128, 8 * TBW], BF16, kind="Internal").ap()

    ctx = ExitStack()
    with ctx:
        P = Prog(nc, ctx)
        for s_ in ("tt", "vv", "aa", "gg", "slc", "glw", "slw", "glx0", "glx1",
                   "slq0", "slq1", "slx", "slx2", "sxb", "gpc", "sq", "so"):
            P.sem(ctx, s_)

        def dma(eng, semname, fn, extra_waits=()):
            P.wait(eng, semname, P.cnt(semname))
            for nm, vl in extra_waits:
                P.wait(eng, nm, vl)
            return P.run(eng, fn, semname, 16)

        def sb(name, shape, dt=F32, c=ctx):
            return c.enter_context(nc.sbuf_tensor(name, shape, dt))

        KT = sb("KT", [128, 8, S], BF16)
        VP = sb("VP", [128, NST, D], BF16)
        wqT = sb("wqT", [128, 8, D], BF16)
        bq = sb("bq", [128, 8]); bk = sb("bk", [128, 8]); bv = sb("bv", [128, 8])
        b4b = sb("b4b", [128, D])
        ident = sb("ident", [128, 128], BF16)
        ssel = sb("ssel", [128, 1])
        ssm1 = sb("ssm1", [128, 1])
        epsln = sb("epsln", [128, 1])
        r127 = sb("r127", [128, 1])
        c1ap = sb("c1ap", [128, 1])
        nc1ap = sb("nc1ap", [128, 1])
        wscv = sb("wscv", [128, 1])
        stats = sb("stats", [128, 128])
        onesf = sb("onesf", [1, 128])

        for dst, src in ((bq, bqh), (bk, bkh), (bv, bvh), (b4b, b4h), (ident, idh)):
            dma("s", "slc", lambda e, d=dst, s_=src: e.dma_start(out=d[:], in_=s_))
        sl_static = P.cnt("slc")

        with ExitStack() as p0:
            wtf = sb("wtf", [128, 8, D], F32, p0)
            u0p = sb("u0p", [128, D], F32, p0)
            u1p = sb("u1p", [128, D], F32, p0)
            u2p = sb("u2p", [128, D], F32, p0)
            u3p = sb("u3p", [128, D], F32, p0)
            ps0 = p0.enter_context(nc.psum_tensor("ps0", [128, 1], F32))

            dma("s", "slc", lambda e: e.dma_start(
                out=wtf[:], in_=wt.rearrange("(c p) o -> p c o", p=128)))
            sl_wtf = P.cnt("slc")

            for wi, src in enumerate((w1, w2)):
                dma("g", "glw", lambda e, wi=wi, src=src: e.dma_start(
                    out=wbh[wi], in_=src))

            P.run("v", lambda e: e.memset(epsln[:], EPS_LN), "vv")
            P.run("v", lambda e: e.memset(c1ap[:], C1), "vv")
            P.run("v", lambda e: e.memset(nc1ap[:], -C1), "vv")
            v = P.run("v", lambda e: e.memset(stats[:, 122:123], 127.0), "vv")
            P.wait("v", "vv", v)
            v = P.run("v", lambda e: e.reciprocal(out=r127[:], in_=stats[:, 122:123]), "vv")
            P.wait("v", "vv", v)
            v = P.run("v", lambda e: e.memset(stats[:, 120:121], 1.25), "vv")
            P.wait("v", "vv", v)
            v = P.run("v", lambda e: e.tensor_scalar(
                out=stats[:, 121:122], in0=stats[:, 120:121], scalar1=C2,
                scalar2=None, op0=ADD), "vv")
            P.wait("v", "vv", v)
            v = P.run("v", lambda e: e.tensor_scalar(
                out=ssel[:], in0=stats[:, 121:122], scalar1=-C2,
                scalar2=None, op0=ADD), "vv")
            vv_ssel = v
            P.wait("v", "vv", v)
            v = P.run("v", lambda e: e.tensor_scalar(
                out=ssm1[:], in0=ssel[:], scalar1=-1.0, scalar2=1.0,
                op0=MUL, op1=ADD), "vv")

            P.wait("v", "slc", sl_wtf)
            for c in range(8):
                P.run("v", lambda e, c=c: e.tensor_reduce(
                    out=stats[:, 96 + c:97 + c], in_=wtf[:, c, :], axis=AX,
                    op=ADD, apply_absolute_value=True), "vv")
            v = P.cnt("vv")
            P.wait("v", "vv", v)
            v = P.run("v", lambda e: e.tensor_reduce(
                out=stats[:, 104:105], in_=stats[:, 96:104], axis=AX, op=ADD), "vv")
            P.wait("g", "vv", v)
            g = P.run("g", lambda e: e.tensor_reduce(
                out=stats[0:1, 105:106], in_=stats[:, 104:105], axis=AXC, op=ADD), "gg")
            for wi, src in enumerate((w3, w4)):
                dma("g", "glw", lambda e, wi=wi, src=src: e.dma_start(
                    out=wbh[wi + 2], in_=src))
            v = P.run("v", lambda e: e.memset(onesf[:], 1.0), "vv")
            P.wait("t", "vv", v)
            P.wait("t", "gg", g)
            t = P.run("t", lambda e: e.matmul(
                ps0[:], onesf[:], stats[0:1, 105:106], start=True, stop=True), "tt")
            P.wait("v", "tt", t)
            v = P.run("v", lambda e: e.tensor_copy(out=stats[:, 106:107], in_=ps0[:]), "vv")
            P.wait("v", "vv", v)
            v = P.run("v", lambda e: e.tensor_scalar(
                out=wscv[:], in0=stats[:, 106:107], scalar1=INVN, scalar2=EPS_Q,
                op0=MUL, op1=ADD), "vv")
            P.wait("v", "vv", v)
            v = P.run("v", lambda e: e.reciprocal(out=stats[:, 107:108], in_=wscv[:]), "vv")
            vv_rws = v

            pA = [u0p, u1p]
            pB = [u2p, u3p]
            P.wait("a", "slc", sl_wtf)
            P.wait("a", "vv", vv_rws)
            chunk_v_done = [0, 0]
            for c in range(8):
                s_ = c % 2
                P.wait("a", "vv", chunk_v_done[s_])
                a = P.run("a", lambda e, c=c, s_=s_: e.activation(
                    out=pA[s_][:], in_=wtf[:, c, :], func=AF.Identity,
                    scale=stats[:, 107:108], bias=0.0), "aa")
                P.wait("a", "aa", a)
                a = P.run("a", lambda e, s_=s_: e.activation(
                    out=pB[s_][:], in_=pA[s_][:], func=AF.Identity,
                    scale=1.0, bias=c1ap[:]), "aa")
                P.wait("a", "aa", a)
                a = P.run("a", lambda e, s_=s_: e.activation(
                    out=pA[s_][:], in_=pB[s_][:], func=AF.Identity,
                    scale=1.0, bias=nc1ap[:]), "aa")
                P.wait("v", "aa", a)
                v = P.run("v", lambda e, s_=s_: e.tensor_scalar(
                    out=pB[s_][:], in0=pA[s_][:], scalar1=C2, scalar2=None,
                    op0=ADD), "vv")
                P.wait("v", "vv", v)
                v = P.run("v", lambda e, s_=s_: e.scalar_tensor_tensor(
                    out=pB[s_][:], in0=pB[s_][:], scalar=-C2, op0=ADD,
                    in1=pA[s_][:], op1=SUB), "vv")
                P.wait("v", "vv", v)
                v = P.run("v", lambda e, s_=s_: e.scalar_tensor_tensor(
                    out=pB[s_][:], in0=pB[s_][:], scalar=ssel[:], op0=MUL,
                    in1=pA[s_][:], op1=ADD), "vv")
                P.wait("v", "vv", v)
                v = P.run("v", lambda e, c=c, s_=s_: e.tensor_scalar(
                    out=wqT[:, c, :], in0=pB[s_][:], scalar1=1.0, scalar2=-1.0,
                    op0=MINO, op1=MAXO), "vv")
                chunk_v_done[s_] = v
            vv_wq = P.cnt("vv")

        def barrier():
            marks = {n: P.cnt(n) for n in P.sems}
            for e in Prog.ENGS:
                for n, v_ in marks.items():
                    P.wait(e, n, v_)
        barrier()

        kt_ready = [0, 0]
        vp_ready = [0, 0]
        qt_stored = [[0] * NTB, [0] * NTB]
        xbh_ready = {}

        phase2_end_tt = 0

        for b in range(B):
            with ExitStack() as p1:
                W1b = sb("W1b%d" % b, [128, 8, D], BF16, p1)
                W2b = sb("W2b%d" % b, [128, 8, D], BF16, p1)
                xtb = [sb(f"xtb{b}_{i}", [128, 8, TBW], BF16, p1) for i in range(2)]
                QTs = sb("QTs%d" % b, [128, 8, TBW], BF16, p1)
                psQ = p1.enter_context(nc.psum_tensor("psQ%d" % b, [128, 4, TBW], F32))
                psK = p1.enter_context(nc.psum_tensor("psK%d" % b, [128, 4, TBW], F32))

                xtb_free = [0, 0]
                bank_free_q = [0] * 4
                bank_free_k = [0] * 4
                qts_free = 0
                gl_xt = [0, 0]
                sl_xt = [0, 0]
                sxb_buf = [0, 0]

                def load_xk(k):
                    if (b, k) in xbh_ready:
                        row = (b * NTB + k) * 128
                        gl_xt[k % 2] = dma(
                            "g", "glx%d" % (k % 2),
                            lambda e, row=row, k=k: e.dma_start(
                                out=xtb[k % 2][:, 0:4, :],
                                in_=xbh[row:row + 128, 0:4 * TBW]
                                .rearrange("p (c t) -> p c t", c=4)),
                            extra_waits=(("tt", xtb_free[k % 2]),
                                         xbh_ready[(b, k)]))
                        sl_xt[k % 2] = dma(
                            "s", "slx2",
                            lambda e, row=row, k=k: e.dma_start(
                                out=xtb[k % 2][:, 4:8, :],
                                in_=xbh[row:row + 128, 4 * TBW:8 * TBW]
                                .rearrange("p (c t) -> p c t", c=4)),
                            extra_waits=(("tt", xtb_free[k % 2]),
                                         xbh_ready[(b, k)]))
                    else:
                        gl_xt[k % 2] = dma(
                            "g", "glx%d" % (k % 2),
                            lambda e, b=b, k=k: e.dma_start(
                                out=xtb[k % 2][:],
                                in_=xf[b * D:(b + 1) * D, k * TBW:(k + 1) * TBW]
                                .rearrange("(c p) t -> p c t", p=128)),
                            extra_waits=(("tt", xtb_free[k % 2]),
                                         ("sxb", sxb_buf[k % 2])))
                        sl_xt[k % 2] = 0
                        row = (b * NTB + k) * 128
                        sxb_buf[k % 2] = dma(
                            "s", "sxb",
                            lambda e, row=row, k=k: e.dma_start(
                                out=xbh[row:row + 128, :]
                                .rearrange("p (c t) -> p c t", c=8),
                                in_=xtb[k % 2][:]),
                            extra_waits=(("glx%d" % (k % 2), gl_xt[k % 2]),))
                        xbh_ready[(b, k)] = ("sxb", sxb_buf[k % 2])

                dma("g", "glw", lambda e: e.dma_start(
                    out=W1b[:, 0:4, :],
                    in_=wbh[0][0:D // 2, :].rearrange("(c p) o -> p c o", p=128)))
                gl_w1 = P.cnt("glw")
                dma("s", "slw", lambda e: e.dma_start(
                    out=W1b[:, 4:8, :],
                    in_=wbh[0][D // 2:D, :].rearrange("(c p) o -> p c o", p=128)))
                sl_w1 = P.cnt("slw")
                load_xk(0)
                dma("g", "glw", lambda e: e.dma_start(
                    out=W2b[:, 0:4, :],
                    in_=wbh[1][0:D // 2, :].rearrange("(c p) o -> p c o", p=128)))
                gl_w2 = P.cnt("glw")
                dma("s", "slw", lambda e: e.dma_start(
                    out=W2b[:, 4:8, :],
                    in_=wbh[1][D // 2:D, :].rearrange("(c p) o -> p c o", p=128)))
                sl_w2 = P.cnt("slw")
                if NTB > 1:
                    load_xk(1)

                for k in range(NTB):
                    evq = []
                    for oc in range(8):
                        P.wait("t", "glw", gl_w1)
                        P.wait("t", "slw", sl_w1)
                        P.wait("t", "glx%d" % (k % 2), gl_xt[k % 2])
                        P.wait("t", "slx2", sl_xt[k % 2])
                        P.wait("t", "vv", bank_free_q[oc % 4])
                        for e_ in range(8):
                            tv = P.run("t", lambda e, oc=oc, e_=e_, k=k: e.matmul(
                                psQ[:, oc % 4, :],
                                W1b[:, e_, oc * 128:(oc + 1) * 128],
                                xtb[k % 2][:, e_, :],
                                start=(e_ == 0), stop=(e_ == 7)),
                                "tt" if e_ == 7 else None)
                        P.wait("v", "tt", tv)
                        P.wait("v", "sq", qts_free)
                        ev = P.run("v", lambda e, oc=oc: e.tensor_scalar(
                            out=QTs[:, oc, :], in0=psQ[:, oc % 4, :],
                            scalar1=bq[:, oc:oc + 1], scalar2=None, op0=ADD), "vv")
                        bank_free_q[oc % 4] = ev
                        evq.append(ev)
                    qt_stored[b][k] = dma(
                        "s", "sq",
                        lambda e, b=b, k=k: e.dma_start(
                            out=qts[b * 1024:(b + 1) * 1024, k * TBW:(k + 1) * TBW]
                            .rearrange("(c p) t -> p c t", p=128),
                            in_=QTs[:]),
                        extra_waits=(("vv", evq[-1]),))
                    qts_free = qt_stored[b][k]

                    for oc in range(8):
                        P.wait("t", "glw", gl_w2)
                        P.wait("t", "slw", sl_w2)
                        P.wait("t", "vv", bank_free_k[oc % 4])
                        for e_ in range(8):
                            tv = P.run("t", lambda e, oc=oc, e_=e_, k=k: e.matmul(
                                psK[:, oc % 4, :],
                                W2b[:, e_, oc * 128:(oc + 1) * 128],
                                xtb[k % 2][:, e_, :],
                                start=(e_ == 0), stop=(e_ == 7)),
                                "tt" if e_ == 7 else None)
                        P.wait("v", "tt", tv)
                        ev = P.run("v", lambda e, oc=oc, k=k: e.tensor_scalar(
                            out=KT[:, oc, k * TBW:(k + 1) * TBW], in0=psK[:, oc % 4, :],
                            scalar1=bk[:, oc:oc + 1], scalar2=None, op0=ADD), "vv")
                        bank_free_k[oc % 4] = ev
                    xtb_free[k % 2] = tv
                    if k + 2 < NTB:
                        load_xk(k + 2)
                kt_ready[b] = P.cnt("vv")
            barrier()

            with ExitStack() as p1v:
                W3b = sb("W3b%d" % b, [128, 8, D], BF16, p1v)
                W4b = sb("W4b%d" % b, [128, 8, D], BF16, p1v)
                xtb = [sb(f"xvb{b}_{i}", [128, 8, TBW], BF16, p1v) for i in range(2)]
                VTb = sb("VTb%d" % b, [128, 8, TBW], BF16, p1v)
                psV = p1v.enter_context(nc.psum_tensor("psV%d" % b, [128, 4, TBW], F32))
                psVp = p1v.enter_context(nc.psum_tensor("psVp%d" % b, [128, 2, TBW], F32))

                xtb_free = [0, 0]
                bank_free_v = [0] * 4
                bank_free_p = [0, 0]
                vtb_free = 0
                gl_xt = [0, 0]
                sl_xt = [0, 0]

                def load_xk(k):
                    row = (b * NTB + k) * 128
                    gl_xt[k % 2] = dma(
                        "g", "glx%d" % (k % 2),
                        lambda e, row=row, k=k: e.dma_start(
                            out=xtb[k % 2][:, 0:4, :],
                            in_=xbh[row:row + 128, 0:4 * TBW]
                            .rearrange("p (c t) -> p c t", c=4)),
                        extra_waits=(("tt", xtb_free[k % 2]),
                                     xbh_ready[(b, k)]))
                    sl_xt[k % 2] = dma(
                        "s", "slx2",
                        lambda e, row=row, k=k: e.dma_start(
                            out=xtb[k % 2][:, 4:8, :],
                            in_=xbh[row:row + 128, 4 * TBW:8 * TBW]
                            .rearrange("p (c t) -> p c t", c=4)),
                        extra_waits=(("tt", xtb_free[k % 2]),
                                     xbh_ready[(b, k)]))

                dma("g", "glw", lambda e: e.dma_start(
                    out=W3b[:, 0:4, :],
                    in_=wbh[2][0:D // 2, :].rearrange("(c p) o -> p c o", p=128)))
                gl_w3 = P.cnt("glw")
                dma("s", "slw", lambda e: e.dma_start(
                    out=W3b[:, 4:8, :],
                    in_=wbh[2][D // 2:D, :].rearrange("(c p) o -> p c o", p=128)))
                sl_w3 = P.cnt("slw")
                load_xk(0)
                dma("g", "glw", lambda e: e.dma_start(
                    out=W4b[:, 0:4, :],
                    in_=wbh[3][0:D // 2, :].rearrange("(c p) o -> p c o", p=128)))
                gl_w4 = P.cnt("glw")
                dma("s", "slw", lambda e: e.dma_start(
                    out=W4b[:, 4:8, :],
                    in_=wbh[3][D // 2:D, :].rearrange("(c p) o -> p c o", p=128)))
                sl_w4 = P.cnt("slw")
                if NTB > 1:
                    load_xk(1)

                for k in range(NTB):
                    vt_evacs = []
                    for oc in range(8):
                        P.wait("t", "glw", gl_w3)
                        P.wait("t", "slw", sl_w3)
                        P.wait("t", "glx%d" % (k % 2), gl_xt[k % 2])
                        P.wait("t", "slx2", sl_xt[k % 2])
                        P.wait("t", "vv", bank_free_v[oc % 4])
                        for e_ in range(8):
                            tv = P.run("t", lambda e, oc=oc, e_=e_, k=k: e.matmul(
                                psV[:, oc % 4, :],
                                W3b[:, e_, oc * 128:(oc + 1) * 128],
                                xtb[k % 2][:, e_, :],
                                start=(e_ == 0), stop=(e_ == 7)),
                                "tt" if e_ == 7 else None)
                        P.wait("v", "tt", tv)
                        P.wait("v", "tt", vtb_free)
                        ev = P.run("v", lambda e, oc=oc: e.tensor_scalar(
                            out=VTb[:, oc, :], in0=psV[:, oc % 4, :],
                            scalar1=bv[:, oc:oc + 1], scalar2=None, op0=ADD), "vv")
                        bank_free_v[oc % 4] = ev
                        vt_evacs.append(ev)
                    xtb_free[k % 2] = tv

                    for tt_ in range(4):
                        for h in range(2):
                            P.wait("t", "glw", gl_w4)
                            P.wait("t", "slw", sl_w4)
                            P.wait("t", "vv", vt_evacs[-1])
                            P.wait("t", "vv", bank_free_p[h])
                            for e_ in range(8):
                                tv = P.run("t", lambda e, tt_=tt_, h=h, e_=e_: e.matmul(
                                    psVp[:, h, :],
                                    VTb[:, e_, tt_ * 128:(tt_ + 1) * 128],
                                    W4b[:, e_, h * TBW:(h + 1) * TBW],
                                    start=(e_ == 0), stop=(e_ == 7)),
                                    "tt" if e_ == 7 else None)
                            P.wait("v", "tt", tv)
                            ev = P.run("v", lambda e, k=k, tt_=tt_, h=h: e.tensor_tensor(
                                out=VP[:, k * 4 + tt_, h * TBW:(h + 1) * TBW],
                                in0=psVp[:, h, :],
                                in1=b4b[:, h * TBW:(h + 1) * TBW], op=ADD), "vv")
                            bank_free_p[h] = ev
                    vtb_free = tv
                    if k + 2 < NTB:
                        load_xk(k + 2)
                vp_ready[b] = P.cnt("vv")
            barrier()

            with ExitStack() as p2:
                scor = sb("scor%d" % b, [128, S], F32, p2)
                Pb = sb("Pb%d" % b, [128, S], BF16, p2)
                PT = sb("PT%d" % b, [128, NST, 128], BF16, p2)
                qtt = sb("qtt%d" % b, [128, 8, 128], BF16, p2)
                attn = sb("attn%d" % b, [128, D], F32, p2)
                x1bufs = [sb(f"x1_{b}_{i}", [128, D], F32, p2) for i in range(2)]
                r2b = sb("r2b%d" % b, [128, D], F32, p2)
                if S >= 3072 + 1024:
                    outbuf = scor[:, 0:1024]
                    u2 = scor[:, 2048:3072]
                    u3 = scor[:, 3072:4096]
                else:
                    outbuf = sb("outb%d" % b, [128, D], F32, p2)
                    u2 = sb("u2b%d" % b, [128, D], F32, p2)
                    u3 = sb("u3b%d" % b, [128, D], F32, p2)
                aqbufs = [sb(f"aqb{b}_{i}", [128, D], BF16, p2) for i in range(2)]
                aqT = sb("aqT%d" % b, [128, 8, 128], BF16, p2)
                psS = p2.enter_context(nc.psum_tensor("psS%d" % b, [128, 4, TBW], F32))
                psT0 = p2.enter_context(nc.psum_tensor("psT0%d" % b, [128, GSZ, 128], BF16))
                psT1 = p2.enter_context(nc.psum_tensor("psT1%d" % b, [128, GSZ, 128], BF16))
                psO = p2.enter_context(nc.psum_tensor("psO%d" % b, [128, 2, TBW], F32))
                psTs = [psT0, psT1]

                sc_stop = {}
                sc_evac_all = {}
                sc_last = {}
                pt_evac = {}
                pt_stop = {}
                pt_last = {-1: 0, -2: 0}
                pv_stop = {}
                pv_q = {}
                attn_evac = {}
                exp_done = {-1: 0}
                rrs_done = {}
                x1_done = {-1: 0}
                aq_done = {-1: 0}
                radd_done = {}
                rfuse = {}
                zc_a_done = {}
                r2h_done = {-1: 0, -2: 0}
                bn2_done = {-1: 0, -2: 0}
                out_done = {-1: 0, -2: 0}
                so_done = {-1: 0, -2: 0}
                ffn_stop = {}
                aqtr_done = {-1: 0}
                aqT_evac = {-1: 0}
                qt_ld = {}
                x_ld = {}
                ysc_done = {-1: 0, -2: 0}
                bn1_done = {}
                psT_free = {0: 0, 1: 0}
                stats_free = {0: (0, 0), 1: (0, 0)}

                def qt_load(i):
                    qt_ld[i] = dma(
                        "g", "slq0",
                        lambda e, b=b, i=i: e.dma_start(
                            out=qtt[:],
                            in_=qts[b * 1024:(b + 1) * 1024, i * 128:(i + 1) * 128]
                            .rearrange("(c p) t -> p c t", p=128)),
                        extra_waits=(("sq", qt_stored[b][NTB - 1]),
                                     ("tt", sc_last.get(i - 1, 0))))

                qt_load(0)

                for it in range(NST + 3):
                    i = it
                    ie = it - 1
                    if_ = it - 3
                    p = i % 2
                    pe = ie % 2
                    sbase = 64 * p
                    sbe = 64 * pe
                    sbf = 64 * (if_ % 2)
                    x1e = x1bufs[ie % 2]
                    x1f = x1bufs[if_ % 2]
                    aqe = aqbufs[ie % 2]
                    aqf = aqbufs[if_ % 2]

                    def sc_block(j):
                        P.wait("t", "slq0", qt_ld[i])
                        P.wait("t", "vv", kt_ready[b])
                        mb0 = sbase + 0
                        pg = i * NSB + j - 4
                        if pg >= 0:
                            P.wait("t", "vv",
                                   sc_evac_all.get((pg // NSB, pg % NSB), 0))
                        for e_ in range(8):
                            tv = P.run("t", lambda e, i=i, j=j, e_=e_: e.matmul(
                                psS[:, j % 4, :],
                                qtt[:, e_, :],
                                KT[:, e_, j * TBW:(j + 1) * TBW],
                                start=(e_ == 0), stop=(e_ == 7)),
                                "tt" if e_ == 7 else None)
                        sc_stop[(i, j)] = tv
                        if j == 0:
                            P.wait("v", "aa", exp_done.get(i - 1, 0))
                            P.wait("v", "vv", out_done.get(i - 4, 0))
                            P.wait("v", "so", so_done.get(i - 4, 0))
                            P.wait("v", "vv", aq_done.get(i - 2, 0))
                            P.wait("v", "vv", rfuse.get((i - 2, 1), 0))
                        if j % 2 == 1 or NSB == 1:
                            P.wait("v", "tt", tv)
                            jp = j // 2
                            j0_, wid = (j - 1, 2) if NSB > 1 else (j, 1)
                            ev = P.run("v", lambda e, j0_=j0_, wid=wid, jp=jp, mb0=mb0: e.tensor_scalar(
                                out=scor[:, j0_ * TBW:(j0_ + wid) * TBW],
                                in0=psS[:, j0_ % 4:j0_ % 4 + wid, :],
                                scalar1=1.0, scalar2=NEG_BIG, op0=MUL, op1=MAXO,
                                accum_out=stats[:, mb0 + jp:mb0 + jp + 1]), "vv")
                            sc_evac_all[(i, j0_)] = ev
                            sc_evac_all[(i, j)] = ev

                    if i < NST:
                        for j in range(NSB // 2):
                            sc_block(j)

                    iq = it - 2
                    if 0 <= iq < NST:
                        sbq = 64 * (iq % 2)
                        x1q = x1bufs[iq % 2]
                        aqq = aqbufs[iq % 2]
                        P.wait("v", "vv", bn1_done[iq])
                        b6 = sbq + 12
                        v = P.run("v", lambda e, b6=b6, sbq=sbq: e.bn_aggr(
                            out=stats[:, sbq + 24:sbq + 26], in_=stats[:, b6:b6 + 12]), "vv")
                        P.wait("a", "vv", v)
                        a = P.run("a", lambda e, sbq=sbq: e.activation(
                            out=stats[:, sbq + 27:sbq + 28], in_=stats[:, sbq + 25:sbq + 26],
                            func=AF.Ln, bias=epsln[:], scale=1.0), "aa")
                        P.wait("a", "aa", a)
                        a = P.run("a", lambda e, sbq=sbq: e.activation(
                            out=stats[:, sbq + 28:sbq + 29], in_=stats[:, sbq + 27:sbq + 28],
                            func=AF.Exp, bias=0.0, scale=-0.5), "aa")
                        rstd_a = a
                        P.wait("v", "aa", rstd_a)
                        P.wait("v", "vv", ysc_done.get(iq - 2, 0))
                        P.run("v", lambda e, sbq=sbq: e.tensor_reduce(
                            out=stats[:, sbq + 58:sbq + 59], in_=stats[:, sbq + 54:sbq + 56],
                            axis=AX, op=MAXO), "vv")
                        v = P.run("v", lambda e, sbq=sbq: e.tensor_reduce(
                            out=stats[:, sbq + 59:sbq + 60], in_=stats[:, sbq + 56:sbq + 58],
                            axis=AX, op=MINO), "vv")
                        P.wait("v", "vv", v)
                        v = P.run("v", lambda e, sbq=sbq: e.tensor_scalar(
                            out=stats[:, sbq + 60:sbq + 62], in0=stats[:, sbq + 58:sbq + 60],
                            scalar1=stats[:, sbq + 24:sbq + 25], scalar2=None,
                            op0=SUB), "vv")
                        P.wait("v", "vv", v)
                        v = P.run("v", lambda e, sbq=sbq: e.tensor_reduce(
                            out=stats[:, sbq + 62:sbq + 63], in_=stats[:, sbq + 60:sbq + 62],
                            axis=AX, op=MAXO, apply_absolute_value=True), "vv")
                        P.wait("v", "vv", v)
                        v = P.run("v", lambda e, sbq=sbq: e.tensor_scalar(
                            out=stats[:, sbq + 31:sbq + 32], in0=stats[:, sbq + 62:sbq + 63],
                            scalar1=stats[:, sbq + 28:sbq + 29], scalar2=EPS_Q,
                            op0=MUL, op1=ADD), "vv")
                        P.wait("v", "vv", v)
                        v = P.run("v", lambda e, sbq=sbq: e.reciprocal(
                            out=stats[:, sbq + 32:sbq + 33], in_=stats[:, sbq + 31:sbq + 32]), "vv")
                        P.wait("v", "vv", v)
                        v = P.run("v", lambda e, sbq=sbq: e.tensor_scalar(
                            out=stats[:, sbq + 33:sbq + 34], in0=stats[:, sbq + 32:sbq + 33],
                            scalar1=127.0, scalar2=stats[:, sbq + 28:sbq + 29],
                            op0=MUL, op1=MUL), "vv")
                        zq_v = v
                        P.wait("v", "vv", zq_v)
                        v = P.run("v", lambda e, sbq=sbq: e.scalar_tensor_tensor(
                            out=stats[:, sbq + 63:sbq + 64], in0=stats[:, sbq + 24:sbq + 25],
                            scalar=-1.0, op0=MUL, in1=stats[:, sbq + 33:sbq + 34],
                            op1=MUL), "vv")
                        nmq_v = v
                        v = P.run("v", lambda e, sbq=sbq: e.scalar_tensor_tensor(
                            out=stats[:, sbq + 29:sbq + 30], in0=stats[:, sbq + 24:sbq + 25],
                            scalar=-1.0, op0=MUL, in1=stats[:, sbq + 28:sbq + 29],
                            op1=MUL), "vv")
                        nmr_v = v
                        P.wait("a", "vv", nmr_v)
                        P.wait("a", "vv", out_done.get(iq - 2, 0))
                        a = P.run("a", lambda e, sbq=sbq, x1q=x1q: e.activation(
                            out=x1q[:], in_=attn[:], func=AF.Identity,
                            scale=stats[:, sbq + 28:sbq + 29],
                            bias=stats[:, sbq + 29:sbq + 30]), "aa")
                        x1_done[iq] = a

                    if if_ >= 0:
                        for h in range(2):
                            P.wait("t", "aa", aqT_evac[if_])
                            P.wait("t", "vv", vv_wq)
                            P.wait("t", "vv", r2h_done.get(if_ - 1, 0))
                            P.wait("t", "vv", rfuse.get((ie - 1, h), 0))
                            for ic in range(8):
                                tv = P.run("t", lambda e, ic=ic, h=h: e.matmul(
                                    psO[:, h, :], aqT[:, ic, :],
                                    wqT[:, ic, h * TBW:(h + 1) * TBW],
                                    start=(ic == 0), stop=(ic == 7)),
                                    "tt" if ic == 7 else None)
                            ffn_stop[(if_, h)] = tv
                        v = P.run("v", lambda e, sbf=sbf: e.tensor_scalar(
                            out=stats[:, sbf + 34:sbf + 35], in0=stats[:, sbf + 31:sbf + 32],
                            scalar1=wscv[:], scalar2=r127[:], op0=MUL, op1=MUL), "vv")
                        ysc_done[if_] = v
                        P.wait("v", "tt", ffn_stop[(if_, 1)])
                        P.wait("v", "vv", v)
                        v2 = P.run("v", lambda e, sbf=sbf, x1f=x1f: e.scalar_tensor_tensor(
                            out=x1f[:], in0=psO[:, 0:2, :],
                            scalar=stats[:, sbf + 34:sbf + 35], op0=MUL,
                            in1=x1f[:], op1=ADD), "vv")
                        r2h_done[if_] = v2

                    if i < NST:
                        for j in range(NSB // 2, NSB):
                            sc_block(j)

                    if i < NST:
                        sc_last[i] = sc_stop[(i, NSB - 1)]
                        P.wait("v", "vv", sc_evac_all[(i, NSB - 1)])
                        nmb = NSB // 2 if NSB > 1 else 1
                        v = P.run("v", lambda e, sbase=sbase, nmb=nmb: e.tensor_reduce(
                            out=stats[:, sbase + 8:sbase + 9],
                            in_=stats[:, sbase:sbase + nmb],
                            axis=AX, op=MAXO), "vv")
                        P.wait("v", "vv", v)
                        v = P.run("v", lambda e, sbase=sbase: e.tensor_scalar(
                            out=stats[:, sbase + 9:sbase + 10],
                            in0=stats[:, sbase + 8:sbase + 9],
                            scalar1=-INV32, scalar2=None, op0=MUL), "vv")
                        negm_v = v
                        P.wait("a", "vv", negm_v)
                        P.wait("a", "tt", pt_last.get(i - 1, 0))
                        a = P.run("a", lambda e, sbase=sbase: e.activation(
                            out=Pb[:], in_=scor[:], func=AF.Exp, scale=INV32,
                            bias=stats[:, sbase + 9:sbase + 10],
                            accum_out=stats[:, sbase + 10:sbase + 11]), "aa")
                        exp_done[i] = a
                        P.wait("v", "aa", a)
                        v = P.run("v", lambda e, sbase=sbase: e.reciprocal(
                            out=stats[:, sbase + 11:sbase + 12],
                            in_=stats[:, sbase + 10:sbase + 11]), "vv")
                        rrs_done[i] = v
                        if i + 1 < NST:
                            qt_load(i + 1)

                    if 0 <= iq < NST:
                        P.wait("v", "vv", nmq_v)
                        P.wait("v", "aa", x1_done[iq])
                        v = P.run("v", lambda e, sbq=sbq, u2=u2: e.tensor_scalar(
                            out=u2[:], in0=attn[:], scalar1=stats[:, sbq + 33:sbq + 34],
                            scalar2=stats[:, sbq + 63:sbq + 64], op0=MUL, op1=ADD), "vv")
                        P.wait("v", "vv", v)
                        v = P.run("v", lambda e, u2=u2: e.tensor_scalar(
                            out=attn[:], in0=u2[:], scalar1=127.0, scalar2=-128.0,
                            op0=MINO, op1=MAXO), "vv")
                        P.wait("v", "vv", v)
                        v = P.run("v", lambda e, u2=u2: e.tensor_scalar(
                            out=u2[:], in0=attn[:], scalar1=C1, scalar2=None,
                            op0=ADD), "vv")
                        P.wait("v", "vv", v)
                        v = P.run("v", lambda e, u2=u2: e.tensor_scalar(
                            out=attn[:], in0=u2[:], scalar1=-C1, scalar2=None,
                            op0=ADD), "vv")
                        P.wait("v", "vv", v)
                        v = P.run("v", lambda e, u2=u2: e.tensor_scalar(
                            out=u2[:], in0=attn[:], scalar1=C2, scalar2=-C2,
                            op0=ADD, op1=ADD), "vv")
                        P.wait("v", "vv", v)
                        v = P.run("v", lambda e, u2=u2, u3=u3: e.tensor_scalar(
                            out=u3[:], in0=u2[:], scalar1=ssel[:], scalar2=None,
                            op0=MUL), "vv")
                        P.wait("v", "vv", v)
                        P.wait("v", "tt", aqtr_done.get(iq - 2, 0))
                        v = P.run("v", lambda e, aqq=aqq, u3=u3: e.scalar_tensor_tensor(
                            out=aqq[:], in0=attn[:], scalar=ssm1[:], op0=MUL,
                            in1=u3[:], op1=ADD), "vv")
                        aq_done[iq] = v

                    def pv_group(h, q):
                        if h == 0 and q == 0:
                            P.wait("t", "vv", vp_ready[b])
                            P.wait("t", "vv", rfuse.get((ie - 1, 0), 0))
                            P.wait("t", "vv", rfuse.get((ie - 1, 1), 0))
                            P.wait("t", "vv", r2h_done.get(if_, 0))
                        P.wait("t", "aa", pt_evac.get((ie, q), 0))
                        tv = None
                        for tc in range(q * GSZ, (q + 1) * GSZ):
                            tv = P.run("t", lambda e, tc=tc, h=h: e.matmul(
                                psO[:, h, :], PT[:, tc, :],
                                VP[:, tc, h * TBW:(h + 1) * TBW],
                                start=(tc == 0), stop=(tc == NST - 1),
                                skip_group_check=True),
                                "tt" if (tc == NST - 1 or tc == (q + 1) * GSZ - 1) else None)
                        if q == NG - 1:
                            pv_stop[(ie, h)] = tv
                        pv_q[(ie, h, q)] = tv

                    def attn_evac_h(h):
                        P.wait("v", "tt", pv_stop[(ie, h)])
                        P.wait("v", "vv", rrs_done[ie])
                        P.wait("v", "vv", aq_done.get(ie - 1, 0))
                        P.wait("v", "slx", x_ld[ie])
                        rf = P.run("v", lambda e, h=h, sbe=sbe: e.scalar_tensor_tensor(
                            out=attn[:, h * TBW:(h + 1) * TBW], in0=psO[:, h, :],
                            scalar=stats[:, sbe + 11:sbe + 12], op0=MUL,
                            in1=r2b[:, h * TBW:(h + 1) * TBW], op1=ADD), "vv")
                        rfuse[(ie, h)] = rf
                        P.wait("v", "vv", rf)
                        b6 = sbe + 12
                        P.run("v", lambda e, h=h, sbe=sbe: e.tensor_reduce(
                            out=stats[:, sbe + 54 + h:sbe + 55 + h],
                            in_=attn[:, h * TBW:(h + 1) * TBW], axis=AX, op=MAXO), "vv")
                        P.run("v", lambda e, h=h, sbe=sbe: e.tensor_reduce(
                            out=stats[:, sbe + 56 + h:sbe + 57 + h],
                            in_=attn[:, h * TBW:(h + 1) * TBW], axis=AX, op=MINO), "vv")
                        return P.run("v", lambda e, h=h, b6=b6: e.bn_stats(
                            out=stats[:, b6 + 6 * h:b6 + 6 * h + 6],
                            in_=attn[:, h * TBW:(h + 1) * TBW]), "vv")

                    def pt_group(g_):
                        P.wait("t", "aa", exp_done[i])
                        P.wait("t", "aa", psT_free[g_ % 2])
                        for kk in range(GSZ):
                            tv = P.run("t", lambda e, g_=g_, kk=kk: e.transpose(
                                psTs[g_ % 2][:, kk, :],
                                Pb[:, (g_ * GSZ + kk) * 128:(g_ * GSZ + kk + 1) * 128],
                                ident[:]), "tt" if kk == GSZ - 1 else None)
                        pt_stop[(i, g_)] = tv
                        P.wait("a", "tt", tv)
                        P.wait("a", "tt", pv_q.get((ie, 1, g_), 0))
                        ev = P.run("a", lambda e, g_=g_: e.activation(
                            out=PT[:, g_ * GSZ:(g_ + 1) * GSZ, :], in_=psTs[g_ % 2][:],
                            func=AF.Identity, bias=0.0, scale=1.0), "aa")
                        pt_evac[(i, g_)] = ev
                        psT_free[g_ % 2] = ev

                    if 0 <= ie and i < NST and NG >= 4:
                        for q in range(NG):
                            pv_group(0, q)
                        attn_evac_h(0)
                        pv_group(1, 0)
                        pt_group(0)
                        pv_group(1, 1)
                        pt_group(1)
                        pv_group(1, 2)
                        pt_group(2)
                        pv_group(1, 3)
                        pt_group(3)
                        bn1_done[ie] = attn_evac_h(1)
                    else:
                        if 0 <= ie < NST:
                            for h in range(2):
                                for q in range(NG):
                                    pv_group(h, q)
                            attn_evac_h(0)
                            bn1_done[ie] = attn_evac_h(1)
                        if i < NST:
                            for g_ in range(NG):
                                pt_group(g_)
                    if i < NST:
                        pt_last[i] = pt_stop[(i, NG - 1)]

                    ic_ = it - 2
                    if 0 <= ic_ < NST:
                        aqc = aqbufs[ic_ % 2]
                        P.wait("t", "vv", aq_done[ic_])
                        for ga in range((8 + GSZ - 1) // GSZ):
                            lo, hi = ga * GSZ, min((ga + 1) * GSZ, 8)
                            P.wait("t", "aa", psT_free[0])
                            for kk in range(lo, hi):
                                tv = P.run("t", lambda e, kk=kk, lo=lo, aqc=aqc: e.transpose(
                                    psT0[:, kk - lo, :], aqc[:, kk * 128:(kk + 1) * 128],
                                    ident[:]), "tt" if kk == hi - 1 else None)
                            P.wait("a", "tt", tv)
                            P.wait("a", "tt", ffn_stop.get((ic_ - 1, 1), 0))
                            ev = P.run("a", lambda e, lo=lo, hi=hi: e.activation(
                                out=aqT[:, lo:hi, :], in_=psT0[:, 0:hi - lo, :],
                                func=AF.Identity, bias=0.0, scale=1.0), "aa")
                            psT_free[0] = ev
                        aqtr_done[ic_] = tv
                        aqT_evac[ic_] = ev

                    if if_ >= 0:
                        P.wait("v", "vv", r2h_done[if_])
                        b62 = sbf + 36
                        P.run("v", lambda e, b62=b62, x1f=x1f: e.bn_stats(
                            out=stats[:, b62:b62 + 6], in_=x1f[:, 0:512]), "vv")
                        v = P.run("v", lambda e, b62=b62, x1f=x1f: e.bn_stats(
                            out=stats[:, b62 + 6:b62 + 12], in_=x1f[:, 512:1024]), "vv")
                        P.wait("v", "vv", v)
                        v = P.run("v", lambda e, b62=b62, sbf=sbf: e.bn_aggr(
                            out=stats[:, sbf + 48:sbf + 50], in_=stats[:, b62:b62 + 12]), "vv")
                        bn2_done[if_] = v
                        P.wait("a", "vv", v)
                        a = P.run("a", lambda e, sbf=sbf: e.activation(
                            out=stats[:, sbf + 51:sbf + 52], in_=stats[:, sbf + 49:sbf + 50],
                            func=AF.Ln, bias=epsln[:], scale=1.0), "aa")
                        P.wait("a", "aa", a)
                        a = P.run("a", lambda e, sbf=sbf: e.activation(
                            out=stats[:, sbf + 52:sbf + 53], in_=stats[:, sbf + 51:sbf + 52],
                            func=AF.Exp, bias=0.0, scale=-0.5), "aa")
                        P.wait("v", "aa", a)
                        v = P.run("v", lambda e, sbf=sbf: e.scalar_tensor_tensor(
                            out=stats[:, sbf + 53:sbf + 54], in0=stats[:, sbf + 48:sbf + 49],
                            scalar=-1.0, op0=MUL, in1=stats[:, sbf + 52:sbf + 53],
                            op1=MUL), "vv")
                        P.wait("v", "vv", v)
                        P.wait("v", "so", so_done.get(if_ - 1, 0))
                        v = P.run("v", lambda e, sbf=sbf, x1f=x1f: e.tensor_scalar(
                            out=outbuf[:], in0=x1f[:],
                            scalar1=stats[:, sbf + 52:sbf + 53],
                            scalar2=stats[:, sbf + 53:sbf + 54],
                            op0=MUL, op1=ADD), "vv")
                        out_done[if_] = v
                        so_done[if_] = dma(
                            "s", "so",
                            lambda e, b=b, if_=if_: e.dma_start(
                                out=out[b * S + if_ * 128:b * S + (if_ + 1) * 128, :],
                                in_=outbuf[:]),
                            extra_waits=(("vv", v),))

                    if i < NST:
                        x_ld[i] = dma(
                            "g", "slx",
                            lambda e, b=b, i=i: e.dma_start(
                                out=r2b[:],
                                in_=xt[b * S + i * 128:b * S + (i + 1) * 128, :]),
                            extra_waits=(("vv", rfuse.get((i - 1, 1), 0)),))

                    nq = 4 * NTB
                    if b + 1 < B and it < nq:
                        kx, qx = it // 4, it % 4
                        row = ((b + 1) * NTB + kx) * 128
                        g = dma("g", "gpc",
                                lambda e, b=b, kx=kx, qx=qx, row=row: e.dma_start(
                                    out=xbh[row:row + 128,
                                            qx * 2 * TBW:(qx + 1) * 2 * TBW]
                                    .rearrange("p (c t) -> p c t", c=2),
                                    in_=xf[(b + 1) * D + qx * 256:
                                           (b + 1) * D + (qx + 1) * 256,
                                           kx * TBW:(kx + 1) * TBW]
                                    .rearrange("(c p) t -> p c t", p=128)))
                        if qx == 3:
                            xbh_ready[(b + 1, kx)] = ("gpc", g)

                phase2_end_tt = P.cnt("tt")
            barrier()

        P.wait("s", "so", P.cnt("so"))

        with nc.Block() as block:
            @block.tensor
            def _(e):
                for f in P.q["t"]:
                    f(e)

            @block.vector
            def _(e):
                for f in P.q["v"]:
                    f(e)

            @block.scalar
            def _(e):
                for f in P.q["a"]:
                    f(e)

            @block.gpsimd
            def _(e):
                for f in P.q["g"]:
                    f(e)

            @block.sync
            def _(e):
                for f in P.q["s"]:
                    f(e)

    return nc


_NC_CACHE = None


def kernel(L0_attn_x, L0_attn_W1, L0_attn_b1, L0_attn_W2, L0_attn_b2,
           L0_attn_W3, L0_attn_b3, L0_attn_W4, L0_attn_b4, L0_ffn_weight):
    global _NC_CACHE
    x = np.ascontiguousarray(np.asarray(L0_attn_x, dtype=np.float32))
    W = [np.ascontiguousarray(np.asarray(w, dtype=np.float32))
         for w in (L0_attn_W1, L0_attn_W2, L0_attn_W3, L0_attn_W4)]
    bs = [np.ascontiguousarray(np.asarray(v, dtype=np.float32))
          for v in (L0_attn_b1, L0_attn_b2, L0_attn_b3, L0_attn_b4)]
    wf = np.ascontiguousarray(np.asarray(L0_ffn_weight, dtype=np.float32))
    wt = np.ascontiguousarray(wf.T)

    bqh = np.ascontiguousarray(bs[0].reshape(8, 128).T)
    bkh = np.ascontiguousarray(bs[1].reshape(8, 128).T)
    bvh = np.ascontiguousarray(bs[2].reshape(8, 128).T)
    b4h = np.ascontiguousarray(np.broadcast_to(bs[3], (128, D)))
    idh = np.eye(128, dtype=ml_dtypes.bfloat16)

    if _NC_CACHE is None:
        _NC_CACHE = build_nc()
    nc = _NC_CACHE

    in_maps = []
    for c in range(8):
        xs = x[2 * c:2 * c + 2]
        in_maps.append({
            "xt": np.ascontiguousarray(xs.reshape(B * S, D)),
            "xf": np.ascontiguousarray(xs.transpose(0, 2, 1).reshape(B * D, S)),
            "w1": W[0], "w2": W[1], "w3": W[2], "w4": W[3],
            "wt": wt, "bqh": bqh, "bkh": bkh, "bvh": bvh, "b4h": b4h,
            "idh": idh,
        })
    res = run_bass_kernel_spmd(nc, in_maps, core_ids=list(range(8)))
    outs = [np.asarray(res.results[c]["out"], dtype=np.float32).reshape(B, S, D)
            for c in range(8)]
    return np.concatenate(outs, axis=0)
